# version 12
# baseline (speedup 1.0000x reference)
import math
import numpy as np
from contextlib import ExitStack

import concourse.bass as bass
import concourse.mybir as mybir
from concourse.bass_utils import run_bass_kernel_spmd

F32 = mybir.dt.float32
BF16 = mybir.dt.bfloat16
AF = mybir.ActivationFunctionType
ALU = mybir.AluOpType
AX = mybir.AxisListType

D = 2048
CC = 16
TOK = 1024
TC = 8
NH = 16
HD = 128
BLK = 256
NB = 4
PRE = 3072
FF = 4096
E = 8
LN_EPS = 1e-5
RMS_EPS = 1e-6


def _slopes():
    start = 2.0 ** (-(2.0 ** -(math.log2(NH) - 3)))
    return [start * (start ** i) for i in range(NH)]


class Ctr:

    def __init__(self, k, name, step=1):
        self.sem = k.es.enter_context(k.nc.semaphore(name))
        self.val = 0
        self.step = step

    def mark(self, instr):
        instr.then_inc(self.sem, self.step)
        self.val += self.step
        return (self.sem, self.val)

    def evt(self):
        return (self.sem, self.val)


class Kb:
    def __init__(self):
        self.nc = bass.Bass("TRN2", target_bir_lowering=False, debug=False)
        self.es = ExitStack()
        self._waited = {}

    def sb(self, name, shape, dtype, es=None):
        return (es or self.es).enter_context(self.nc.sbuf_tensor(name, shape, dtype))

    def ps(self, name, shape, dtype):
        return self.es.enter_context(self.nc.psum_tensor(name, shape, dtype))

    def wait(self, eng, evt):
        if evt is None:
            return
        if isinstance(evt, (list, tuple)) and evt and isinstance(evt[0], (list, tuple)):
            for e in evt:
                self.wait(eng, e)
            return
        sem, val = evt
        if val <= 0:
            return
        key = (id(eng), id(sem))
        if self._waited.get(key, -1) >= val:
            return
        eng.wait_ge(sem, val)
        self._waited[key] = val


class Pool:

    def __init__(self, k, n=7):
        self.k = k
        self.banks = [k.ps(f"pb{i}", [128, 512], F32) for i in range(n)]
        self.evts = [None] * n
        self.i = 0
        self.n = n
        self.skip = set()

    def get(self, eng):
        while True:
            b = self.i % self.n
            self.i += 1
            if b not in self.skip:
                break
        self.k.wait(eng, self.evts[b])
        return b, self.banks[b]

    def reserve(self, eng):
        b, bank = self.get(eng)
        self.skip.add(b)
        return b, bank

    def put(self, b, evt):
        self.skip.discard(b)
        self.evts[b] = evt


class Stream:

    def __init__(self, k, name, cols, dtype=BF16, nbuf=2, es=None, initial_free=None):
        self.k = k
        self.bufs = [k.sb(f"{name}{i}", [128, cols], dtype, es=es) for i in range(nbuf)]
        self.sems = [k.es.enter_context(k.nc.semaphore(f"{name}s{i}")) for i in range(nbuf)]
        self.cnt = [0] * nbuf
        self.free = [list(initial_free) if initial_free else None for _ in range(nbuf)]
        self.i = 0

    def load(self, src_ap, cols=None):
        s = self.i % len(self.bufs)
        self.i += 1
        self.k.wait(self.k.nc.gpsimd, self.free[s])
        dst = self.bufs[s][:, :cols] if cols else self.bufs[s][:, :]
        ins = self.k.nc.gpsimd.dma_start(out=dst, in_=src_ap)
        ins.then_inc(self.sems[s], 16)
        self.cnt[s] += 16
        return s, (self.sems[s], self.cnt[s])

    def done(self, s, evt):
        self.free[s] = evt


def build_nc(debug=False):
    k = Kb()
    nc = k.nc
    TE, DV, AC, SY, GP = nc.tensor, nc.vector, nc.scalar, nc.sync, nc.gpsimd

    din = {}
    for name, shape in [
        ("xo", [TOK, D]),
        ("wq", [D, D]), ("wk", [D, D]), ("wv", [D, D]),
        ("wga", [D, D]), ("wout", [D, D]), ("wgf", [D, E]),
        ("w1", [E, D, FF]), ("w2", [E, FF, D]),
        ("ws1", [D, FF]), ("ws2", [FF, D]),
        ("tbl", [NH, 128, 768]),
        ("kdc", [128, 2 * NH]), ("nwc", [128, NH]), ("cf", [128, 64]),
        ("idm", [128, 128]), ("onc", [128, 1]), ("onr", [1, 128]),
    ]:
        din[name] = nc.dram_tensor(name, shape, F32, kind="ExternalInput").ap()
    out = nc.dram_tensor("out", [TOK, D], F32, kind="ExternalOutput").ap()
    agin = nc.dram_tensor("agin", [128, NH * HD], BF16)
    agout = nc.dram_tensor("agout", [4 * 128, NH * HD], BF16)

    dbg = {}
    if debug:
        for name, shape, dt in [
            ("d_hfm", [128, CC * TOK], BF16), ("d_obf", [128, NH * TOK], BF16),
            ("d_S", [128, NH * HD], F32), ("d_x1", [128, TC * D], F32),
            ("d_wrt", [128, 64], F32), ("d_r", [1, TOK], F32),
        ]:
            dbg[name] = nc.dram_tensor(name, shape, dt, kind="ExternalOutput").ap()

    xo_r = din["xo"].rearrange("(tc p) c -> p tc c", p=128)
    out_r = out.rearrange("(tc p) c -> p tc c", p=128)
    wq_r = din["wq"].rearrange("(cc p) n -> p cc n", p=128)
    wk_r = din["wk"].rearrange("(cc p) n -> p cc n", p=128)
    wv_r = din["wv"].rearrange("(cc p) n -> p cc n", p=128)
    wga_r = din["wga"].rearrange("(cc p) n -> p cc n", p=128)
    wout_r = din["wout"].rearrange("(hc p) n -> p hc n", p=128)
    wgf_r = din["wgf"].rearrange("(cc p) n -> p cc n", p=128)
    w1_r = din["w1"].rearrange("e (cc p) f -> e p cc f", p=128)
    w2_r = din["w2"].rearrange("e (fc p) c -> e p fc c", p=128)
    ws1_r = din["ws1"].rearrange("(cc p) f -> p cc f", p=128)
    ws2_r = din["ws2"].rearrange("(fc p) c -> p fc c", p=128)
    tbl_r = din["tbl"]

    slopes = _slopes()
    bdk = [math.exp(-s * BLK) for s in slopes]

    xres = k.sb("xres", [128, TC * D], F32)
    hfm = k.sb("hfm", [128, CC * TOK], BF16)
    Sst = k.sb("Sst", [128, NH * HD], F32)
    kdc_s = k.sb("kdc_s", [128, 2 * NH], F32)
    nwc_s = k.sb("nwc_s", [128, NH], F32)
    cf_s = k.sb("cf_s", [128, 64], F32)
    idm_s = k.sb("idm_s", [128, 128], BF16)
    onc_s = k.sb("onc_s", [128, 1], BF16)
    onr_s = k.sb("onr_s", [1, 128], F32)
    stats = k.sb("stats", [128, 8], F32)
    rsc = k.sb("rsc", [128, 32], F32)
    r_row = k.sb("r_row", [1, TOK], F32)
    rcol_s = k.sb("rcol_s", [128, TC], F32)
    rcol2_s = k.sb("rcol2_s", [128, TC], F32)
    wrt = k.sb("wrt", [128, 64], F32)
    qq_scr = k.sb("qq_scr", [128, 2 * BLK], BF16)
    c1_scr = k.sb("c1_scr", [128, 4 * BLK], BF16)
    sbf_scr = k.sb("sbf_scr", [128, 2 * HD], BF16)
    osq_scr = k.sb("osq_scr", [128, TOK], BF16)
    g_scr = k.sb("g_scr", [128, TOK], BF16)
    htm_scr = k.sb("htm_scr", [128, D], BF16)

    pool = Pool(k, n=6)
    tbs_ps = [k.ps(f"tb{i}", [128, 1024], BF16) for i in range(2)]
    tb_evt = [None, None]
    tb_i = [0]

    te = Ctr(k, "te")
    dv = Ctr(k, "dv")
    ac = Ctr(k, "ac")
    syl = Ctr(k, "syl", 16)
    gpl = Ctr(k, "gpl", 16)
    sto = Ctr(k, "sto", 16)

    def snap():
        return [te.evt(), dv.evt(), ac.evt()]

    xres_v = xres.ap().rearrange("p (tc c) -> p tc c", c=D)

    SY.dma_start(out=xres_v[:, :, :], in_=xo_r[:, :, :]).then_inc(syl.sem, 16)
    SY.dma_start(out=kdc_s[:, :], in_=din["kdc"][:, :]).then_inc(syl.sem, 16)
    SY.dma_start(out=nwc_s[:, :], in_=din["nwc"][:, :]).then_inc(syl.sem, 16)
    SY.dma_start(out=onr_s[:, :], in_=din["onr"][:, :]).then_inc(syl.sem, 16)
    SY.dma_start(out=cf_s[:, :], in_=din["cf"][:, :]).then_inc(syl.sem, 16)
    syl.val += 80
    e_xres0 = e_kdc = e_nwc = e_onr = e_cf = syl.evt()
    GP.dma_start(out=idm_s[:, :], in_=din["idm"][:, :]).then_inc(gpl.sem, 16)
    GP.dma_start(out=onc_s[:, :], in_=din["onc"][:, :]).then_inc(gpl.sem, 16)
    gpl.val += 32
    e_idm = e_onc = gpl.evt()
    e_S0 = dv.mark(DV.memset(Sst[:, :], 0.0))

    def sdv():
        k.wait(DV, dv.evt())

    def sac():
        k.wait(AC, ac.evt())

    def dvo(instr):
        return dv.mark(instr)

    def ln_chunk(x_ap, out_bf_ap, guard=None, f32_ap=None):
        k.wait(DV, guard)
        for i in range(4):
            dvo(DV.bn_stats(rsc[:, i * 6:(i + 1) * 6], x_ap[:, i * 512:(i + 1) * 512]))
        sdv()
        dvo(DV.bn_aggr(stats[:, 0:2], rsc[:, 0:24]))
        sdv()
        dvo(DV.tensor_scalar_mul(stats[:, 2:3], stats[:, 0:1], -1.0))
        e = dvo(DV.tensor_scalar_add(stats[:, 3:4], stats[:, 1:2], LN_EPS))
        k.wait(AC, e)
        e1 = ac.mark(AC.activation(stats[:, 5:6], stats[:, 3:4], AF.Ln))
        sac()
        e2 = ac.mark(AC.activation(stats[:, 6:7], stats[:, 5:6], AF.Exp, scale=-0.5))
        k.wait(DV, e2)
        if f32_ap is not None:
            dvo(DV.tensor_scalar(f32_ap, x_ap, stats[:, 2:3], stats[:, 6:7], ALU.add, ALU.mult))
            sdv()
            return dvo(DV.tensor_copy(out_bf_ap, f32_ap))
        return dvo(DV.tensor_scalar(out_bf_ap, x_ap, stats[:, 2:3], stats[:, 6:7],
                                    ALU.add, ALU.mult))

    def transpose16(src_bf, dst_view3, tcol, e_src):
        e_te = None
        e_dv = None
        for half in range(2):
            ti = tb_i[0] % 2
            tb_i[0] += 1
            tb = tbs_ps[ti]
            k.wait(TE, tb_evt[ti])
            k.wait(TE, e_src)
            k.wait(TE, e_idm)
            for j in range(8):
                cc = half * 8 + j
                ins = TE.transpose(tb[:, j * 128:(j + 1) * 128],
                                   src_bf[:, cc * 128:(cc + 1) * 128], idm_s[:, :])
            e_te = te.mark(ins)
            k.wait(DV, e_te)
            e_dv = dv.mark(DV.tensor_copy(
                dst_view3[:, half * 8:half * 8 + 8, tcol:tcol + 128],
                tb.ap().rearrange("p (j t) -> p j t", t=128)))
            tb_evt[ti] = e_dv
        return e_dv, e_te

    hfm_v = hfm.ap().rearrange("p (cc t) -> p cc t", t=TOK)

    k.wait(DV, e_xres0)
    for tc in range(TC):
        e_h = ln_chunk(xres[:, tc * D:(tc + 1) * D], htm_scr[:, :], guard=te.evt())
        transpose16(htm_scr, hfm_v, tc * 128, e_h)
    e_hfm = dv.evt()

    big_arena = ExitStack()
    obf = k.sb("obf", [128, NH * TOK], BF16, es=big_arena)
    wqkv = Stream(k, "wqkv", CC * 256, nbuf=3, es=big_arena, initial_free=snap())
    tbs = Stream(k, "tbs", 2 * 768, nbuf=2, es=big_arena, initial_free=snap())
    with ExitStack() as arena:
        ktm1 = k.sb("ktm1", [128, TC * BLK], BF16, es=arena)
        vtm1 = k.sb("vtm1", [128, TC * BLK], BF16, es=arena)
        kscr = k.sb("kscr", [128, 256], BF16, es=arena)
        wkv1 = Stream(k, "wkv1", CC * 256, es=arena, initial_free=snap())
        e_kscr = None
        for hg in range(8):
            skp, ekp = wkv1.load(wk_r[:, :, hg * 256:(hg + 1) * 256])
            svp, evp = wkv1.load(wv_r[:, :, hg * 256:(hg + 1) * 256])
            last_k = None
            for tsub in range(TC):
                b, bank = pool.get(TE)
                k.wait(TE, ekp)
                k.wait(TE, e_hfm)
                for cc in range(CC):
                    ins = TE.matmul(bank[:, 0:256],
                                    hfm[:, cc * TOK + tsub * 128: cc * TOK + tsub * 128 + 128],
                                    wkv1.bufs[skp][:, cc * 256:(cc + 1) * 256],
                                    start=(cc == 0), stop=(cc == CC - 1))
                e_mm = te.mark(ins)
                k.wait(AC, e_mm)
                k.wait(AC, e_kscr)
                e_s = ac.mark(AC.activation(kscr[:, :], bank[:, 0:256], AF.Silu))
                k.wait(DV, e_s)
                k.wait(DV, e_kdc)
                last = None
                for hh in range(2):
                    h = hg * 2 + hh
                    last = DV.tensor_scalar_mul(
                        ktm1[:, tsub * BLK + hh * 128: tsub * BLK + hh * 128 + 128],
                        kscr[:, hh * 128:(hh + 1) * 128],
                        kdc_s[:, 2 * h + (tsub % 2): 2 * h + (tsub % 2) + 1])
                e_kscr = dv.mark(last)
                last_k = e_kscr
                pool.put(b, e_kscr)
            wkv1.done(skp, last_k)
            last_v = None
            for tsub in range(TC):
                b, bank = pool.get(TE)
                k.wait(TE, evp)
                k.wait(TE, e_hfm)
                for cc in range(CC):
                    ins = TE.matmul(bank[:, 0:256],
                                    hfm[:, cc * TOK + tsub * 128: cc * TOK + tsub * 128 + 128],
                                    wkv1.bufs[svp][:, cc * 256:(cc + 1) * 256],
                                    start=(cc == 0), stop=(cc == CC - 1))
                e_mm = te.mark(ins)
                k.wait(AC, e_mm)
                last_v = ac.mark(AC.activation(
                    vtm1[:, tsub * BLK:(tsub + 1) * BLK], bank[:, 0:256], AF.Silu))
                pool.put(b, last_v)
            wkv1.done(svp, last_v)
            e_kv = [dv.evt(), ac.evt()]
            for hh in range(2):
                h = hg * 2 + hh
                for blk in range(NB):
                    b, bank = pool.get(TE)
                    k.wait(TE, e_kv)
                    for nsub in range(2):
                        tsub = blk * 2 + nsub
                        ins = TE.matmul(bank[:, 0:128],
                                        ktm1[:, tsub * BLK + hh * 128: tsub * BLK + hh * 128 + 128],
                                        vtm1[:, tsub * BLK + hh * 128: tsub * BLK + hh * 128 + 128],
                                        start=(nsub == 0), stop=(nsub == 1))
                    e_mm = te.mark(ins)
                    k.wait(DV, e_mm)
                    k.wait(DV, e_S0)
                    sdv()
                    e_u = dv.mark(DV.scalar_tensor_tensor(
                        Sst[:, h * HD:(h + 1) * HD], Sst[:, h * HD:(h + 1) * HD],
                        bdk[h], bank[:, 0:128], ALU.mult, ALU.add))
                    pool.put(b, e_u)

    pf = {}
    pf[(0, 'q')] = wqkv.load(wq_r[:, :, 0:256])
    pf[(0, 'k')] = wqkv.load(wk_r[:, :, 0:256])
    pf[(0, 'v')] = wqkv.load(wv_r[:, :, 0:256])
    pf[(0, 't')] = tbs.load(tbl_r[0:2, :, :].rearrange("h p c -> p h c"))

    agsem = Ctr(k, "agsem", 16)
    ccs = Ctr(k, "ccs", 1)
    gsem = [Ctr(k, f"gsem{i}", 16) for i in range(2)]
    k.wait(GP, dv.evt())
    e_agst = agsem.mark(GP.dma_start(out=agin[:, :], in_=Sst[:, :]))
    k.wait(GP, e_agst)
    e_cc = ccs.mark(GP.collective_compute(
        "AllGather", ALU.bypass,
        replica_groups=[[0, 1, 2, 3], [4, 5, 6, 7]],
        ins=[agin.ap().opt()], outs=[agout.ap().opt()]))
    gev = None
    k.wait(SY, e_cc)
    for jj in range(4):
        k.wait(SY, gev)
        gsem[0].mark(SY.dma_start(out=osq_scr[:, :], in_=agout[jj * 128:(jj + 1) * 128, 0:1024]))
        e_g = gsem[1].mark(SY.dma_start(out=g_scr[:, :], in_=agout[jj * 128:(jj + 1) * 128, 1024:2048]))
        k.wait(DV, gsem[0].evt())
        k.wait(DV, e_g)
        k.wait(DV, e_cf)
        k.wait(DV, agsem.evt())
        sdv()
        last = None
        for h in range(NH):
            col = cf_s[:, jj * NH + h: jj * NH + h + 1]
            gb = osq_scr[:, h * HD:(h + 1) * HD] if h < 8 else g_scr[:, (h - 8) * HD:(h - 7) * HD]
            if jj == 0:
                last = DV.tensor_scalar_mul(Sst[:, h * HD:(h + 1) * HD], gb, col)
            else:
                last = DV.scalar_tensor_tensor(
                    Sst[:, h * HD:(h + 1) * HD], gb, col,
                    Sst[:, h * HD:(h + 1) * HD], ALU.mult, ALU.add)
        gev = dv.mark(last)

    if debug:
        k.wait(SY, dv.evt())
        _e = sto.mark(SY.dma_start(out=dbg["d_S"][:, :], in_=Sst[:, :]))
        for _eng in (TE, DV, AC):
            k.wait(_eng, _e)

    with big_arena as arena:
        with ExitStack() as sub1:
            qfm = k.sb("qfm", [128, 2 * TOK], BF16, es=sub1)
            kfm = k.sb("kfm", [128, 2 * TOK], BF16, es=sub1)
            vtm = k.sb("vtma", [128, TC * BLK], BF16, es=sub1)
            kptm = k.sb("kptma", [128, TC * BLK], BF16, es=sub1)


            prev_ret = None
            for g in range(8):
                h0 = 2 * g
                sq, eq_ = pf.pop((g, 'q'))
                sk, ek_ = pf.pop((g, 'k'))
                stb, etb = pf.pop((g, 't'))

                for (ss, ee, dst) in [(sq, eq_, qfm), (sk, ek_, kfm)]:
                    last_ev = None
                    for hh in range(2):
                        for th in range(2):
                            b, bank = pool.get(TE)
                            k.wait(TE, ee)
                            k.wait(TE, e_hfm)
                            for cc in range(CC):
                                ins = TE.matmul(bank[:, 0:512],
                                          wqkv.bufs[ss][:, cc * 256 + hh * 128: cc * 256 + hh * 128 + 128],
                                          hfm[:, cc * TOK + th * 512: cc * TOK + th * 512 + 512],
                                          start=(cc == 0), stop=(cc == CC - 1))
                            e_mm = te.mark(ins)
                            k.wait(AC, e_mm)
                            last_ev = ac.mark(AC.activation(
                                dst[:, hh * TOK + th * 512: hh * TOK + th * 512 + 512],
                                bank[:, 0:512], AF.Silu))
                            pool.put(b, last_ev)
                    wqkv.done(ss, last_ev)
                    if ss == sq and g + 1 < 8:
                        pf[(g + 1, 'q')] = wqkv.load(wq_r[:, :, (h0 + 2) * 128:(h0 + 4) * 128])
                        pf[(g + 1, 't')] = tbs.load(tbl_r[h0 + 2:h0 + 4, :, :].rearrange("h p c -> p h c"))
                    elif ss == sk and g + 1 < 8:
                        pf[(g + 1, 'k')] = wqkv.load(wk_r[:, :, (h0 + 2) * 128:(h0 + 4) * 128])
                e_qk = ac.evt()

                sv, ev_ = pf.pop((g, 'v'))
                last_ev = None
                for tsub in range(TC):
                    b, bank = pool.get(TE)
                    k.wait(TE, ev_)
                    k.wait(TE, e_hfm)
                    for cc in range(CC):
                        ins = TE.matmul(bank[:, 0:BLK],
                                  hfm[:, cc * TOK + tsub * 128: cc * TOK + tsub * 128 + 128],
                                  wqkv.bufs[sv][:, cc * BLK:(cc + 1) * BLK],
                                  start=(cc == 0), stop=(cc == CC - 1))
                    e_mm = te.mark(ins)
                    k.wait(AC, e_mm)
                    last_ev = ac.mark(AC.activation(
                        vtm[:, tsub * BLK:(tsub + 1) * BLK], bank[:, 0:BLK], AF.Silu))
                    pool.put(b, last_ev)
                wqkv.done(sv, last_ev)
                if g + 1 < 8:
                    pf[(g + 1, 'v')] = wqkv.load(wv_r[:, :, (h0 + 2) * 128:(h0 + 4) * 128])
                e_v = ac.evt()

                for tsub in range(TC):
                    ti = tb_i[0] % 2
                    tb_i[0] += 1
                    tb = tbs_ps[ti]
                    k.wait(TE, tb_evt[ti])
                    k.wait(TE, e_qk)
                    for hh in range(2):
                        ins = TE.transpose(tb[:, hh * 128:(hh + 1) * 128],
                                           kfm[:, hh * TOK + tsub * 128: hh * TOK + tsub * 128 + 128],
                                           idm_s[:, :])
                    e_t = te.mark(ins)
                    k.wait(DV, e_t)
                    k.wait(DV, e_kdc)
                    last = None
                    for hh in range(2):
                        h = h0 + hh
                        last = DV.tensor_scalar_mul(
                            kptm[:, tsub * BLK + hh * 128: tsub * BLK + hh * 128 + 128],
                            tb[:, hh * 128:(hh + 1) * 128],
                            kdc_s[:, 2 * h + (tsub % 2): 2 * h + (tsub % 2) + 1])
                    e_kp = dv.mark(last)
                    tb_evt[ti] = e_kp
                e_kp = dv.evt()

                tbuf = tbs.bufs[stb]
                iters = [(hh, blk) for hh in range(2) for blk in range(NB)]
                c1ev = [None] * len(iters)
                qqev = [None] * len(iters)
                c1bank = [None] * len(iters)

                def stage_a(i):
                    hh, blk = iters[i]
                    par = i % 2
                    k.wait(DV, etb)
                    k.wait(DV, bprev[0])
                    qqev[i] = dv.mark(DV.tensor_tensor(
                        qq_scr[:, par * BLK:(par + 1) * BLK],
                        qfm[:, hh * TOK + blk * BLK: hh * TOK + (blk + 1) * BLK],
                        tbuf[:, hh * 768: hh * 768 + 256], ALU.mult))
                    b1, bank1 = pool.get(TE)
                    k.wait(TE, e_qk)
                    for nsub in range(2):
                        ins = TE.matmul(bank1[:, nsub * 256:(nsub + 1) * 256],
                                  kfm[:, hh * TOK + blk * BLK + nsub * 128:
                                      hh * TOK + blk * BLK + nsub * 128 + 128],
                                  qfm[:, hh * TOK + blk * BLK: hh * TOK + (blk + 1) * BLK],
                                  start=True, stop=True)
                    e_c1 = te.mark(ins)
                    k.wait(DV, e_c1)
                    last = None
                    for nsub in range(2):
                        last = DV.tensor_tensor(
                            c1_scr[:, par * 512 + nsub * 256: par * 512 + (nsub + 1) * 256],
                            bank1[:, nsub * 256:(nsub + 1) * 256],
                            tbuf[:, hh * 768 + 256 + nsub * 256:
                                 hh * 768 + 256 + (nsub + 1) * 256],
                            ALU.mult)
                    c1ev[i] = dv.mark(last)
                    pool.put(b1, c1ev[i])

                bprev = [prev_ret]

                sbfev = [None] * len(iters)

                def stage_b_pre(i):
                    hh, blk = iters[i]
                    par = i % 2
                    h = h0 + hh
                    sdv()
                    sbfev[i] = dv.mark(DV.tensor_copy(
                        sbf_scr[:, par * HD:(par + 1) * HD], Sst[:, h * HD:(h + 1) * HD]))

                def stage_b(i):
                    hh, blk = iters[i]
                    par = i % 2
                    h = h0 + hh
                    e_sbf = sbfev[i]
                    b2, bank2 = pool.get(TE)
                    k.wait(TE, c1ev[i])
                    k.wait(TE, e_sbf)
                    k.wait(TE, qqev[i])
                    k.wait(TE, e_v)
                    for nsub in range(2):
                        tsub = blk * 2 + nsub
                        TE.matmul(bank2[:, 0:256],
                                  vtm[:, tsub * BLK + hh * 128: tsub * BLK + hh * 128 + 128],
                                  c1_scr[:, par * 512 + nsub * 256: par * 512 + (nsub + 1) * 256],
                                  start=(nsub == 0), stop=False)
                    ins = TE.matmul(bank2[:, 0:256], sbf_scr[:, par * HD:(par + 1) * HD],
                                    qq_scr[:, par * BLK:(par + 1) * BLK],
                                    start=False, stop=True)
                    e_o = te.mark(ins)
                    bprev[0] = e_o
                    k.wait(DV, e_o)
                    e_oe = dv.mark(DV.tensor_copy(
                        obf[:, h * TOK + blk * BLK: h * TOK + (blk + 1) * BLK],
                        bank2[:, 0:256]))
                    pool.put(b2, e_oe)
                    b3, bank3 = pool.get(TE)
                    k.wait(TE, e_kp)
                    for nsub in range(2):
                        tsub = blk * 2 + nsub
                        ins = TE.matmul(bank3[:, 0:128],
                                  kptm[:, tsub * BLK + hh * 128: tsub * BLK + hh * 128 + 128],
                                  vtm[:, tsub * BLK + hh * 128: tsub * BLK + hh * 128 + 128],
                                  start=(nsub == 0), stop=(nsub == 1))
                    e_cb = te.mark(ins)
                    k.wait(DV, e_cb)
                    sdv()
                    e_su = dv.mark(DV.scalar_tensor_tensor(
                        Sst[:, h * HD:(h + 1) * HD], Sst[:, h * HD:(h + 1) * HD],
                        bdk[h], bank3[:, 0:128], ALU.mult, ALU.add))
                    pool.put(b3, e_su)

                stage_a(0)
                for i in range(len(iters)):
                    stage_b_pre(i)
                    if i + 1 < len(iters):
                        stage_a(i + 1)
                    stage_b(i)
                prev_ret = bprev[0]
                tbs.done(stb, dv.evt())
            e_obf = dv.evt()

            if debug:
                k.wait(SY, e_obf)
                sto.mark(SY.dma_start(out=dbg["d_hfm"][:, :], in_=hfm[:, :]))
                _e = sto.mark(SY.dma_start(out=dbg["d_obf"][:, :], in_=obf[:, :]))
                for _eng in (TE, DV, AC):
                    k.wait(_eng, _e)

            br0, bankr0 = pool.reserve(TE)
            br1, bankr1 = pool.reserve(TE)
            e_sqprev = None
            e_gprev = None
            pg = wqkv.load(wga_r[:, :, 0:256])
            for h in range(NH):
                if h % 2 == 0:
                    sg, eg_ = pg
                    if h + 2 < NH:
                        pg = wqkv.load(wga_r[:, :, (h + 2) * 128:(h + 4) * 128])
                hh = h % 2
                k.wait(DV, e_obf)
                k.wait(DV, e_sqprev)
                sdv()
                e_sq = dv.mark(DV.tensor_tensor(
                    osq_scr[:, :], obf[:, h * TOK:(h + 1) * TOK],
                    obf[:, h * TOK:(h + 1) * TOK], ALU.mult))
                k.wait(TE, e_sq)
                k.wait(TE, e_onc)
                TE.matmul(bankr0[0:1, 0:512], onc_s[:, :], osq_scr[:, 0:512],
                          start=(h == 0), stop=(h == NH - 1))
                ins = TE.matmul(bankr1[0:1, 0:512], onc_s[:, :], osq_scr[:, 512:1024],
                                start=(h == 0), stop=(h == NH - 1))
                e_sqprev = te.mark(ins)
                for th in range(2):
                    b, bank = pool.get(TE)
                    k.wait(TE, eg_)
                    k.wait(TE, e_hfm)
                    for cc in range(CC):
                        ins = TE.matmul(bank[:, 0:512],
                                  wqkv.bufs[sg][:, cc * 256 + hh * 128: cc * 256 + hh * 128 + 128],
                                  hfm[:, cc * TOK + th * 512: cc * TOK + th * 512 + 512],
                                  start=(cc == 0), stop=(cc == CC - 1))
                    e_mm = te.mark(ins)
                    k.wait(AC, e_mm)
                    k.wait(AC, e_gprev)
                    e_g = ac.mark(AC.activation(g_scr[:, th * 512:(th + 1) * 512],
                                                bank[:, 0:512], AF.Sigmoid))
                    pool.put(b, e_g)
                if hh == 1:
                    wqkv.done(sg, ac.evt())
                k.wait(DV, ac.evt())
                k.wait(DV, e_nwc)
                sdv()
                e_gprev = dv.mark(DV.scalar_tensor_tensor(
                    obf[:, h * TOK:(h + 1) * TOK], obf[:, h * TOK:(h + 1) * TOK],
                    nwc_s[:, h:h + 1], g_scr[:, :], ALU.mult, ALU.mult))
            e_ofin = dv.evt()
            k.wait(DV, e_sqprev)
            dv.mark(DV.tensor_scalar_mul(r_row[0:1, 0:512], bankr0[0:1, 0:512], 1.0 / D))
            dv.mark(DV.tensor_scalar_mul(r_row[0:1, 512:1024], bankr1[0:1, 0:512], 1.0 / D))
            sdv()
            e_rr = dv.mark(DV.tensor_scalar_add(r_row[0:1, :], r_row[0:1, :], RMS_EPS))
            pool.put(br0, e_rr)
            pool.put(br1, e_rr)
            bb, bankb = pool.get(TE)
            k.wait(TE, e_rr)
            k.wait(TE, e_onr)
            for tsub in range(TC):
                ins = TE.matmul(bankb[:, tsub:tsub + 1],
                                r_row[0:1, tsub * 128:(tsub + 1) * 128],
                                onr_s[0:1, 0:1], start=True, stop=True)
            e_b = te.mark(ins)
            k.wait(DV, e_b)
            e_vc = dv.mark(DV.tensor_copy(rcol2_s[:, 0:TC], bankb[:, 0:TC]))
            pool.put(bb, e_vc)
            k.wait(AC, e_vc)
            ac.mark(AC.activation(rcol_s[:, 0:TC], rcol2_s[:, 0:TC], AF.Ln))
            sac()
            e_rc = ac.mark(AC.activation(rcol2_s[:, 0:TC], rcol_s[:, 0:TC], AF.Exp, scale=-0.5))

        wos = Stream(k, "wos", CC * 256, es=arena, initial_free=snap())
        for cs in range(8):
            so, eo_ = wos.load(wout_r[:, :, cs * 256:(cs + 1) * 256])
            for tsub in range(TC):
                b, bank = pool.get(TE)
                k.wait(TE, eo_)
                k.wait(TE, e_ofin)
                for hc in range(NH):
                    ins = TE.matmul(bank[:, 0:256],
                              obf[:, hc * TOK + tsub * 128: hc * TOK + tsub * 128 + 128],
                              wos.bufs[so][:, hc * 256:(hc + 1) * 256],
                              start=(hc == 0), stop=(hc == NH - 1))
                e_mm = te.mark(ins)
                k.wait(DV, e_mm)
                k.wait(DV, e_rc)
                sdv()
                e_acc = dv.mark(DV.scalar_tensor_tensor(
                    xres[:, tsub * D + cs * 256: tsub * D + cs * 256 + 256],
                    bank[:, 0:256], rcol2_s[:, tsub:tsub + 1],
                    xres[:, tsub * D + cs * 256: tsub * D + cs * 256 + 256],
                    ALU.mult, ALU.add))
                pool.put(b, e_acc)
            wos.done(so, dv.evt())
    e_x1 = dv.evt()

    if debug:
        k.wait(SY, e_x1)
        _e = sto.mark(SY.dma_start(out=dbg["d_x1"][:, :], in_=xres[:, :]))
        for _eng in (TE, DV, AC):
            k.wait(_eng, _e)

    with ExitStack() as arena:
        hf32_scr = k.sb("hf32_scr", [128, D], F32, es=arena)
        hlotm_scr = k.sb("hlotm", [128, D], BF16, es=arena)
        wgf_s = k.sb("wgf_s", [128, CC * E], F32, es=arena)
        whi = k.sb("whi", [128, CC * E], BF16, es=arena)
        wlo = k.sb("wlo", [128, CC * E], BF16, es=arena)

        k.wait(SY, snap())
        wgfsem = Ctr(k, "wgfsem", 16)
        e_wgf = wgfsem.mark(SY.dma_start(out=wgf_s[:, :], in_=wgf_r[:, :, :]))

        with ExitStack() as sub1:
            hlo = k.sb("hlo", [128, CC * TOK], BF16, es=sub1)
            hlo_v = hlo.ap().rearrange("p (cc t) -> p cc t", t=TOK)

            for tc in range(TC):
                k.wait(DV, e_x1)
                e_h = ln_chunk(xres[:, tc * D:(tc + 1) * D], htm_scr[:, :],
                               guard=te.evt(), f32_ap=hf32_scr[:, :])
                sdv()
                e_lo = dv.mark(DV.tensor_tensor(hlotm_scr[:, :], hf32_scr[:, :],
                                                htm_scr[:, :], ALU.subtract))
                transpose16(htm_scr, hfm_v, tc * 128, e_h)
                transpose16(hlotm_scr, hlo_v, tc * 128, e_lo)
            e_hfm2 = dv.evt()

            k.wait(DV, e_wgf)
            e_whi = dv.mark(DV.tensor_copy(whi[:, :], wgf_s[:, :]))
            sdv()
            e_wlo = dv.mark(DV.tensor_tensor(wlo[:, :], wgf_s[:, :], whi[:, :], ALU.subtract))

            k.wait(DV, ac.evt())
            for tsub in range(TC):
                b, bank = pool.get(TE)
                k.wait(TE, e_whi)
                k.wait(TE, e_wlo)
                k.wait(TE, e_hfm2)
                first = True
                for pa, (lh, rh) in enumerate([(hfm, whi), (hfm, wlo), (hlo, whi)]):
                    for cc in range(CC):
                        ins = TE.matmul(bank[:, 0:E],
                                  lh[:, cc * TOK + tsub * 128: cc * TOK + tsub * 128 + 128],
                                  rh[:, cc * E:(cc + 1) * E],
                                  start=first, stop=(pa == 2 and cc == CC - 1))
                        first = False
                e_lg = te.mark(ins)
                L = bank[:, 0:E]
                k.wait(DV, e_lg)
                sdv()
                dv.mark(DV.reduce_max(stats[:, 0:1], L, axis=AX.X))
                sdv()
                dv.mark(DV.tensor_scalar(rsc[:, 0:E], L, stats[:, 0:1], None, ALU.is_equal))
                sdv()
                dv.mark(DV.scalar_tensor_tensor(rsc[:, 8:16], rsc[:, 0:E], -1e30, L,
                                                ALU.mult, ALU.add))
                sdv()
                dv.mark(DV.reduce_max(stats[:, 1:2], rsc[:, 8:16], axis=AX.X))
                sdv()
                dv.mark(DV.tensor_scalar(rsc[:, 0:E], L, stats[:, 1:2], None, ALU.is_ge))
                dv.mark(DV.tensor_scalar_mul(stats[:, 2:3], stats[:, 0:1], -1.0))
                sdv()
                e_nm1 = dv.mark(DV.tensor_tensor(stats[:, 3:4], stats[:, 1:2],
                                                 stats[:, 0:1], ALU.subtract))
                k.wait(AC, e_nm1)
                ac.mark(AC.activation(rsc[:, 16:24], L, AF.Exp, bias=stats[:, 2:3]))
                e_e2 = ac.mark(AC.activation(stats[:, 4:5], stats[:, 3:4], AF.Exp))
                k.wait(DV, e_e2)
                dv.mark(DV.tensor_scalar_add(stats[:, 5:6], stats[:, 4:5], 1.0))
                sdv()
                dv.mark(DV.reciprocal(stats[:, 6:7], stats[:, 5:6]))
                dv.mark(DV.tensor_tensor(rsc[:, 24:32], rsc[:, 16:24], rsc[:, 0:E], ALU.mult))
                sdv()
                e_w = dv.mark(DV.tensor_scalar_mul(wrt[:, tsub * E:(tsub + 1) * E],
                                                   rsc[:, 24:32], stats[:, 6:7]))
                pool.put(b, e_w)
            e_wrt = dv.evt()

            if debug:
                k.wait(SY, e_wrt)
                _e = sto.mark(SY.dma_start(out=dbg["d_wrt"][:, :], in_=wrt[:, :]))
                for _eng in (TE, DV, AC):
                    k.wait(_eng, _e)

        h1 = [k.sb(f"h1_{i}", [128, 8 * TOK], BF16, es=arena) for i in range(2)]
        w1s = Stream(k, "w1s", CC * 128, es=arena, initial_free=snap())
        w2s = Stream(k, "w2s", 8 * 512, es=arena, initial_free=snap())
        h1ev = [None, None]

        tasks = [(e, fs) for e in range(E + 1) for fs in range(4)]
        s1done = {}

        def emit_s1(i):
            e, fs = tasks[i]
            w1src = ws1_r if e == E else w1_r[e]
            sl = i % 2
            hb = h1[sl]
            last_s1 = None
            for fc in range(8):
                s1, e1_ = w1s.load(w1src[:, :, fs * 1024 + fc * 128: fs * 1024 + fc * 128 + 128])
                bA, bankA = pool.get(TE)
                bB, bankB = pool.get(TE)
                k.wait(TE, e1_)
                k.wait(TE, e_hfm2)
                k.wait(TE, h1ev[sl])
                for cc in range(CC):
                    TE.matmul(bankA[:, 0:512],
                              w1s.bufs[s1][:, cc * 128:(cc + 1) * 128],
                              hfm[:, cc * TOK: cc * TOK + 512],
                              start=(cc == 0), stop=(cc == CC - 1))
                    ins = TE.matmul(bankB[:, 0:512],
                              w1s.bufs[s1][:, cc * 128:(cc + 1) * 128],
                              hfm[:, cc * TOK + 512: cc * TOK + 1024],
                              start=(cc == 0), stop=(cc == CC - 1))
                e_mm = te.mark(ins)
                k.wait(AC, e_mm)
                eA = ac.mark(AC.activation(
                    hb[:, fc * TOK: fc * TOK + 512], bankA[:, 0:512], AF.Silu))
                last_s1 = ac.mark(AC.activation(
                    hb[:, fc * TOK + 512: fc * TOK + 1024], bankB[:, 0:512], AF.Silu))
                pool.put(bA, eA)
                pool.put(bB, last_s1)
                w1s.done(s1, last_s1)
            s1done[i] = last_s1

        def emit_s2(i):
            e, fs = tasks[i]
            w2src = ws2_r if e == E else w2_r[e]
            shared = (e == E)
            sl = i % 2
            hb = h1[sl]
            e_h1 = s1done[i]
            last_te = None
            for cs in range(4):
                s2, e2_ = w2s.load(w2src[:, fs * 8:(fs + 1) * 8, cs * 512:(cs + 1) * 512])
                for tp in range(4):
                    t0, t1 = 2 * tp, 2 * tp + 1
                    bA, bankA = pool.get(TE)
                    bB, bankB = pool.get(TE)
                    k.wait(TE, e2_)
                    k.wait(TE, e_h1)
                    for fc in range(8):
                        TE.matmul(bankA[:, 0:512],
                                  hb[:, fc * TOK + t0 * 128: fc * TOK + t0 * 128 + 128],
                                  w2s.bufs[s2][:, fc * 512:(fc + 1) * 512],
                                  start=(fc == 0), stop=(fc == 7))
                        ins = TE.matmul(bankB[:, 0:512],
                                  hb[:, fc * TOK + t1 * 128: fc * TOK + t1 * 128 + 128],
                                  w2s.bufs[s2][:, fc * 512:(fc + 1) * 512],
                                  start=(fc == 0), stop=(fc == 7))
                    e_mm = te.mark(ins)
                    last_te = e_mm
                    k.wait(DV, e_mm)
                    sdv()
                    for tsub, b, bank in ((t0, bA, bankA), (t1, bB, bankB)):
                        xr = xres[:, tsub * D + cs * 512: tsub * D + cs * 512 + 512]
                        if shared:
                            e_acc = dv.mark(DV.tensor_tensor(xr, xr, bank[:, 0:512], ALU.add))
                        else:
                            k.wait(DV, e_wrt)
                            e_acc = dv.mark(DV.scalar_tensor_tensor(
                                xr, bank[:, 0:512], wrt[:, tsub * E + e: tsub * E + e + 1],
                                xr, ALU.mult, ALU.add))
                        pool.put(b, e_acc)
                w2s.done(s2, dv.evt())
            h1ev[sl] = last_te

        emit_s1(0)
        for i in range(len(tasks)):
            if i + 1 < len(tasks):
                emit_s1(i + 1)
            emit_s2(i)

    k.wait(SY, dv.evt())
    e_out = sto.mark(SY.dma_start(out=out_r[:, :, :], in_=xres_v[:, :, :]))
    k.wait(SY, e_out)

    k.es.close()
    return nc



def _host_inputs(L0_attn_x, Wqkv, Wg, Wout, norm_w, fWg, W1, W2, Ws1, Ws2):
    x = np.ascontiguousarray(np.asarray(L0_attn_x, dtype=np.float32).reshape(2 * 4096, D))
    Wqkv = np.asarray(Wqkv, dtype=np.float32).reshape(D, NH, 3, HD)
    wq = np.ascontiguousarray(Wqkv[:, :, 0, :].reshape(D, D))
    wk = np.ascontiguousarray(Wqkv[:, :, 1, :].reshape(D, D))
    wv = np.ascontiguousarray(Wqkv[:, :, 2, :].reshape(D, D))

    slopes = np.array(_slopes(), dtype=np.float64)
    m = np.arange(BLK, dtype=np.float64)
    qd = np.exp(-slopes[:, None] * (m[None, :] + 1.0))
    kd = np.exp(-slopes[:, None] * (BLK - 1.0 - m[None, :]))
    idx = m[None, :, None] - m[None, None, :]
    dd = np.where(idx >= 0, np.exp(-slopes[:, None, None] * idx), 0.0)
    ddT = np.ascontiguousarray(np.swapaxes(dd, 1, 2))

    tbl = np.zeros((NH, 128, 768), dtype=np.float32)
    tbl[:, :, 0:256] = qd[:, None, :]
    tbl[:, :, 256:512] = ddT[:, 0:128, :]
    tbl[:, :, 512:768] = ddT[:, 128:256, :]

    kdc = np.zeros((128, 2 * NH), dtype=np.float32)
    for h in range(NH):
        kdc[:, 2 * h] = kd[h, 0:128]
        kdc[:, 2 * h + 1] = kd[h, 128:256]

    nwc = np.ascontiguousarray(np.asarray(norm_w, dtype=np.float32).reshape(NH, 128).T)

    shared = dict(
        wq=wq, wk=wk, wv=wv,
        wga=np.ascontiguousarray(np.asarray(Wg, dtype=np.float32)),
        wout=np.ascontiguousarray(np.asarray(Wout, dtype=np.float32)),
        wgf=np.ascontiguousarray(np.asarray(fWg, dtype=np.float32)),
        w1=np.ascontiguousarray(np.asarray(W1, dtype=np.float32)),
        w2=np.ascontiguousarray(np.asarray(W2, dtype=np.float32)),
        ws1=np.ascontiguousarray(np.asarray(Ws1, dtype=np.float32)),
        ws2=np.ascontiguousarray(np.asarray(Ws2, dtype=np.float32)),
        tbl=tbl, kdc=kdc, nwc=nwc,
        idm=np.eye(128, dtype=np.float32),
        onc=np.ones((128, 1), dtype=np.float32),
        onr=np.ones((1, 128), dtype=np.float32),
    )
    bd = np.exp(-slopes * BLK)
    in_maps = []
    for c in range(8):
        b, j = c // 4, c % 4
        row0 = b * 4096
        xo = np.ascontiguousarray(x[row0 + j * TOK: row0 + (j + 1) * TOK])
        cf = np.zeros((128, 64), dtype=np.float32)
        for jj in range(j):
            cf[:, jj * NH:(jj + 1) * NH] = (bd ** (NB * (j - 1 - jj)))[None, :]
        in_maps.append(dict(shared, xo=xo, cf=cf))
    return in_maps


_NC_CACHE = {}


def kernel(L0_attn_x, L0_attn_Wqkv, L0_attn_Wg, L0_attn_Wout, L0_attn_norm_w,
           L0_ffn_Wg, L0_ffn_W1, L0_ffn_W2, L0_ffn_Ws1, L0_ffn_Ws2):
    in_maps = _host_inputs(L0_attn_x, L0_attn_Wqkv, L0_attn_Wg, L0_attn_Wout,
                           L0_attn_norm_w, L0_ffn_Wg, L0_ffn_W1, L0_ffn_W2,
                           L0_ffn_Ws1, L0_ffn_Ws2)
    if "nc" not in _NC_CACHE:
        _NC_CACHE["nc"] = build_nc(debug=False)
    nc = _NC_CACHE["nc"]
    res = run_bass_kernel_spmd(nc, in_maps, core_ids=list(range(8)))
    shards = [np.asarray(r["out"]) for r in res.results]
    full = np.concatenate(shards, axis=0).reshape(2, 4096, D)
    return full.astype(np.float32)


# version 13
# speedup vs baseline: 1.0114x; 1.0114x over previous
import math
import numpy as np
from contextlib import ExitStack

import concourse.bass as bass
import concourse.mybir as mybir
from concourse.bass_utils import run_bass_kernel_spmd

F32 = mybir.dt.float32
BF16 = mybir.dt.bfloat16
AF = mybir.ActivationFunctionType
ALU = mybir.AluOpType
AX = mybir.AxisListType

D = 2048
CC = 16
TOK = 1024
TC = 8
NH = 16
HD = 128
BLK = 256
NB = 4
PRE = 3072
FF = 4096
E = 8
LN_EPS = 1e-5
RMS_EPS = 1e-6


def _slopes():
    start = 2.0 ** (-(2.0 ** -(math.log2(NH) - 3)))
    return [start * (start ** i) for i in range(NH)]


class Ctr:

    def __init__(self, k, name, step=1):
        self.sem = k.es.enter_context(k.nc.semaphore(name))
        self.val = 0
        self.step = step

    def mark(self, instr):
        instr.then_inc(self.sem, self.step)
        self.val += self.step
        return (self.sem, self.val)

    def evt(self):
        return (self.sem, self.val)


class Kb:
    def __init__(self):
        self.nc = bass.Bass("TRN2", target_bir_lowering=False, debug=False)
        self.es = ExitStack()
        self._waited = {}

    def sb(self, name, shape, dtype, es=None):
        return (es or self.es).enter_context(self.nc.sbuf_tensor(name, shape, dtype))

    def ps(self, name, shape, dtype):
        return self.es.enter_context(self.nc.psum_tensor(name, shape, dtype))

    def wait(self, eng, evt):
        if evt is None:
            return
        if isinstance(evt, (list, tuple)) and evt and isinstance(evt[0], (list, tuple)):
            for e in evt:
                self.wait(eng, e)
            return
        sem, val = evt
        if val <= 0:
            return
        key = (id(eng), id(sem))
        if self._waited.get(key, -1) >= val:
            return
        eng.wait_ge(sem, val)
        self._waited[key] = val


class Pool:

    def __init__(self, k, n=7):
        self.k = k
        self.banks = [k.ps(f"pb{i}", [128, 512], F32) for i in range(n)]
        self.evts = [None] * n
        self.i = 0
        self.n = n
        self.skip = set()

    def get(self, eng):
        while True:
            b = self.i % self.n
            self.i += 1
            if b not in self.skip:
                break
        self.k.wait(eng, self.evts[b])
        return b, self.banks[b]

    def reserve(self, eng):
        b, bank = self.get(eng)
        self.skip.add(b)
        return b, bank

    def put(self, b, evt):
        self.skip.discard(b)
        self.evts[b] = evt


class Stream:

    def __init__(self, k, name, cols, dtype=BF16, nbuf=2, es=None, initial_free=None):
        self.k = k
        self.bufs = [k.sb(f"{name}{i}", [128, cols], dtype, es=es) for i in range(nbuf)]
        self.sems = [k.es.enter_context(k.nc.semaphore(f"{name}s{i}")) for i in range(nbuf)]
        self.cnt = [0] * nbuf
        self.free = [list(initial_free) if initial_free else None for _ in range(nbuf)]
        self.i = 0

    def load(self, src_ap, cols=None):
        s = self.i % len(self.bufs)
        self.i += 1
        self.k.wait(self.k.nc.gpsimd, self.free[s])
        dst = self.bufs[s][:, :cols] if cols else self.bufs[s][:, :]
        ins = self.k.nc.gpsimd.dma_start(out=dst, in_=src_ap)
        ins.then_inc(self.sems[s], 16)
        self.cnt[s] += 16
        return s, (self.sems[s], self.cnt[s])

    def done(self, s, evt):
        self.free[s] = evt


def build_nc(debug=False):
    k = Kb()
    nc = k.nc
    TE, DV, AC, SY, GP = nc.tensor, nc.vector, nc.scalar, nc.sync, nc.gpsimd

    din = {}
    for name, shape in [
        ("xo", [TOK, D]),
        ("wq", [D, D]), ("wk", [D, D]), ("wv", [D, D]),
        ("wga", [D, D]), ("wout", [D, D]), ("wgf", [D, E]),
        ("w1", [E, D, FF]), ("w2", [E, FF, D]),
        ("ws1", [D, FF]), ("ws2", [FF, D]),
        ("tbl", [NH, 128, 768]),
        ("kdc", [128, 2 * NH]), ("nwc", [128, NH]), ("cf", [128, 64]),
        ("idm", [128, 128]), ("onc", [128, 1]), ("onr", [1, 128]),
    ]:
        din[name] = nc.dram_tensor(name, shape, F32, kind="ExternalInput").ap()
    out = nc.dram_tensor("out", [TOK, D], F32, kind="ExternalOutput").ap()
    agin0 = nc.dram_tensor("agin0", [128, 8 * HD], BF16)
    agout0 = nc.dram_tensor("agout0", [4 * 128, 8 * HD], BF16)
    agin1 = nc.dram_tensor("agin1", [128, 8 * HD], BF16)
    agout1 = nc.dram_tensor("agout1", [4 * 128, 8 * HD], BF16)

    dbg = {}
    if debug:
        for name, shape, dt in [
            ("d_hfm", [128, CC * TOK], BF16), ("d_obf", [128, NH * TOK], BF16),
            ("d_S", [128, NH * HD], F32), ("d_x1", [128, TC * D], F32),
            ("d_wrt", [128, 64], F32), ("d_r", [1, TOK], F32),
        ]:
            dbg[name] = nc.dram_tensor(name, shape, dt, kind="ExternalOutput").ap()

    xo_r = din["xo"].rearrange("(tc p) c -> p tc c", p=128)
    out_r = out.rearrange("(tc p) c -> p tc c", p=128)
    wq_r = din["wq"].rearrange("(cc p) n -> p cc n", p=128)
    wk_r = din["wk"].rearrange("(cc p) n -> p cc n", p=128)
    wv_r = din["wv"].rearrange("(cc p) n -> p cc n", p=128)
    wga_r = din["wga"].rearrange("(cc p) n -> p cc n", p=128)
    wout_r = din["wout"].rearrange("(hc p) n -> p hc n", p=128)
    wgf_r = din["wgf"].rearrange("(cc p) n -> p cc n", p=128)
    w1_r = din["w1"].rearrange("e (cc p) f -> e p cc f", p=128)
    w2_r = din["w2"].rearrange("e (fc p) c -> e p fc c", p=128)
    ws1_r = din["ws1"].rearrange("(cc p) f -> p cc f", p=128)
    ws2_r = din["ws2"].rearrange("(fc p) c -> p fc c", p=128)
    tbl_r = din["tbl"]

    slopes = _slopes()
    bdk = [math.exp(-s * BLK) for s in slopes]

    xres = k.sb("xres", [128, TC * D], F32)
    hfm = k.sb("hfm", [128, CC * TOK], BF16)
    Sst = k.sb("Sst", [128, NH * HD], F32)
    kdc_s = k.sb("kdc_s", [128, 2 * NH], F32)
    nwc_s = k.sb("nwc_s", [128, NH], F32)
    cf_s = k.sb("cf_s", [128, 64], F32)
    idm_s = k.sb("idm_s", [128, 128], BF16)
    onc_s = k.sb("onc_s", [128, 1], BF16)
    onr_s = k.sb("onr_s", [1, 128], F32)
    stats = k.sb("stats", [128, 8], F32)
    rsc = k.sb("rsc", [128, 32], F32)
    r_row = k.sb("r_row", [1, TOK], F32)
    rcol_s = k.sb("rcol_s", [128, TC], F32)
    rcol2_s = k.sb("rcol2_s", [128, TC], F32)
    wrt = k.sb("wrt", [128, 64], F32)
    qq_scr = k.sb("qq_scr", [128, 2 * BLK], BF16)
    c1_scr = k.sb("c1_scr", [128, 4 * BLK], BF16)
    sbf_scr = k.sb("sbf_scr", [128, 2 * HD], BF16)
    osq_scr = k.sb("osq_scr", [128, TOK], BF16)
    g_scr = k.sb("g_scr", [128, TOK], BF16)
    htm_scr = k.sb("htm_scr", [128, D], BF16)

    pool = Pool(k, n=6)
    tbs_ps = [k.ps(f"tb{i}", [128, 1024], BF16) for i in range(2)]
    tb_evt = [None, None]
    tb_i = [0]

    te = Ctr(k, "te")
    dv = Ctr(k, "dv")
    ac = Ctr(k, "ac")
    syl = Ctr(k, "syl", 16)
    gpl = Ctr(k, "gpl", 16)
    sto = Ctr(k, "sto", 16)

    def snap():
        return [te.evt(), dv.evt(), ac.evt()]

    xres_v = xres.ap().rearrange("p (tc c) -> p tc c", c=D)

    SY.dma_start(out=xres_v[:, :, :], in_=xo_r[:, :, :]).then_inc(syl.sem, 16)
    SY.dma_start(out=kdc_s[:, :], in_=din["kdc"][:, :]).then_inc(syl.sem, 16)
    SY.dma_start(out=nwc_s[:, :], in_=din["nwc"][:, :]).then_inc(syl.sem, 16)
    SY.dma_start(out=onr_s[:, :], in_=din["onr"][:, :]).then_inc(syl.sem, 16)
    SY.dma_start(out=cf_s[:, :], in_=din["cf"][:, :]).then_inc(syl.sem, 16)
    syl.val += 80
    e_xres0 = e_kdc = e_nwc = e_onr = e_cf = syl.evt()
    GP.dma_start(out=idm_s[:, :], in_=din["idm"][:, :]).then_inc(gpl.sem, 16)
    GP.dma_start(out=onc_s[:, :], in_=din["onc"][:, :]).then_inc(gpl.sem, 16)
    gpl.val += 32
    e_idm = e_onc = gpl.evt()
    e_S0 = dv.mark(DV.memset(Sst[:, :], 0.0))

    def sdv():
        k.wait(DV, dv.evt())

    def sac():
        k.wait(AC, ac.evt())

    def dvo(instr):
        return dv.mark(instr)

    def ln_chunk(x_ap, out_bf_ap, guard=None, f32_ap=None):
        k.wait(DV, guard)
        for i in range(4):
            dvo(DV.bn_stats(rsc[:, i * 6:(i + 1) * 6], x_ap[:, i * 512:(i + 1) * 512]))
        sdv()
        dvo(DV.bn_aggr(stats[:, 0:2], rsc[:, 0:24]))
        sdv()
        dvo(DV.tensor_scalar_mul(stats[:, 2:3], stats[:, 0:1], -1.0))
        e = dvo(DV.tensor_scalar_add(stats[:, 3:4], stats[:, 1:2], LN_EPS))
        k.wait(AC, e)
        e1 = ac.mark(AC.activation(stats[:, 5:6], stats[:, 3:4], AF.Ln))
        sac()
        e2 = ac.mark(AC.activation(stats[:, 6:7], stats[:, 5:6], AF.Exp, scale=-0.5))
        k.wait(DV, e2)
        if f32_ap is not None:
            dvo(DV.tensor_scalar(f32_ap, x_ap, stats[:, 2:3], stats[:, 6:7], ALU.add, ALU.mult))
            sdv()
            return dvo(DV.tensor_copy(out_bf_ap, f32_ap))
        return dvo(DV.tensor_scalar(out_bf_ap, x_ap, stats[:, 2:3], stats[:, 6:7],
                                    ALU.add, ALU.mult))

    def transpose16(src_bf, dst_view3, tcol, e_src):
        e_te = None
        e_dv = None
        for half in range(2):
            ti = tb_i[0] % 2
            tb_i[0] += 1
            tb = tbs_ps[ti]
            k.wait(TE, tb_evt[ti])
            k.wait(TE, e_src)
            k.wait(TE, e_idm)
            for j in range(8):
                cc = half * 8 + j
                ins = TE.transpose(tb[:, j * 128:(j + 1) * 128],
                                   src_bf[:, cc * 128:(cc + 1) * 128], idm_s[:, :])
            e_te = te.mark(ins)
            k.wait(DV, e_te)
            e_dv = dv.mark(DV.tensor_copy(
                dst_view3[:, half * 8:half * 8 + 8, tcol:tcol + 128],
                tb.ap().rearrange("p (j t) -> p j t", t=128)))
            tb_evt[ti] = e_dv
        return e_dv, e_te

    hfm_v = hfm.ap().rearrange("p (cc t) -> p cc t", t=TOK)

    k.wait(DV, e_xres0)
    for tc in range(TC):
        e_h = ln_chunk(xres[:, tc * D:(tc + 1) * D], htm_scr[:, :], guard=te.evt())
        transpose16(htm_scr, hfm_v, tc * 128, e_h)
    e_hfm = dv.evt()

    big_arena = ExitStack()
    obf = k.sb("obf", [128, NH * TOK], BF16, es=big_arena)
    wqkv = Stream(k, "wqkv", CC * 256, nbuf=3, es=big_arena, initial_free=snap())
    tbs = Stream(k, "tbs", 2 * 768, nbuf=2, es=big_arena, initial_free=snap())
    agsem = Ctr(k, "agsem", 16)
    ccs = Ctr(k, "ccs", 1)
    gsem = [Ctr(k, f"gsem{i}", 16) for i in range(2)]
    with ExitStack() as arena:
        ktm1 = k.sb("ktm1", [128, TC * BLK], BF16, es=arena)
        vtm1 = k.sb("vtm1", [128, TC * BLK], BF16, es=arena)
        kscr = k.sb("kscr", [128, 256], BF16, es=arena)
        wkv1 = Stream(k, "wkv1", CC * 256, es=arena, initial_free=snap())
        e_kscr = None
        for hg in range(8):
            skp, ekp = wkv1.load(wk_r[:, :, hg * 256:(hg + 1) * 256])
            svp, evp = wkv1.load(wv_r[:, :, hg * 256:(hg + 1) * 256])
            last_k = None
            for tsub in range(TC):
                b, bank = pool.get(TE)
                k.wait(TE, ekp)
                k.wait(TE, e_hfm)
                for cc in range(CC):
                    ins = TE.matmul(bank[:, 0:256],
                                    hfm[:, cc * TOK + tsub * 128: cc * TOK + tsub * 128 + 128],
                                    wkv1.bufs[skp][:, cc * 256:(cc + 1) * 256],
                                    start=(cc == 0), stop=(cc == CC - 1))
                e_mm = te.mark(ins)
                k.wait(AC, e_mm)
                k.wait(AC, e_kscr)
                e_s = ac.mark(AC.activation(kscr[:, :], bank[:, 0:256], AF.Silu))
                k.wait(DV, e_s)
                k.wait(DV, e_kdc)
                last = None
                for hh in range(2):
                    h = hg * 2 + hh
                    last = DV.tensor_scalar_mul(
                        ktm1[:, tsub * BLK + hh * 128: tsub * BLK + hh * 128 + 128],
                        kscr[:, hh * 128:(hh + 1) * 128],
                        kdc_s[:, 2 * h + (tsub % 2): 2 * h + (tsub % 2) + 1])
                e_kscr = dv.mark(last)
                last_k = e_kscr
                pool.put(b, e_kscr)
            wkv1.done(skp, last_k)
            last_v = None
            for tsub in range(TC):
                b, bank = pool.get(TE)
                k.wait(TE, evp)
                k.wait(TE, e_hfm)
                for cc in range(CC):
                    ins = TE.matmul(bank[:, 0:256],
                                    hfm[:, cc * TOK + tsub * 128: cc * TOK + tsub * 128 + 128],
                                    wkv1.bufs[svp][:, cc * 256:(cc + 1) * 256],
                                    start=(cc == 0), stop=(cc == CC - 1))
                e_mm = te.mark(ins)
                k.wait(AC, e_mm)
                last_v = ac.mark(AC.activation(
                    vtm1[:, tsub * BLK:(tsub + 1) * BLK], bank[:, 0:256], AF.Silu))
                pool.put(b, last_v)
            wkv1.done(svp, last_v)
            e_kv = [dv.evt(), ac.evt()]
            for hh in range(2):
                h = hg * 2 + hh
                for blk in range(NB):
                    b, bank = pool.get(TE)
                    k.wait(TE, e_kv)
                    for nsub in range(2):
                        tsub = blk * 2 + nsub
                        ins = TE.matmul(bank[:, 0:128],
                                        ktm1[:, tsub * BLK + hh * 128: tsub * BLK + hh * 128 + 128],
                                        vtm1[:, tsub * BLK + hh * 128: tsub * BLK + hh * 128 + 128],
                                        start=(nsub == 0), stop=(nsub == 1))
                    e_mm = te.mark(ins)
                    k.wait(DV, e_mm)
                    k.wait(DV, e_S0)
                    sdv()
                    e_u = dv.mark(DV.scalar_tensor_tensor(
                        Sst[:, h * HD:(h + 1) * HD], Sst[:, h * HD:(h + 1) * HD],
                        bdk[h], bank[:, 0:128], ALU.mult, ALU.add))
                    pool.put(b, e_u)
            if hg == 3:
                k.wait(GP, dv.evt())
                e_st0 = agsem.mark(GP.dma_start(out=agin0[:, :], in_=Sst[:, 0:8 * HD]))
                k.wait(GP, e_st0)
                e_cc0 = ccs.mark(GP.collective_compute(
                    "AllGather", ALU.bypass,
                    replica_groups=[[0, 1, 2, 3], [4, 5, 6, 7]],
                    ins=[agin0.ap().opt()], outs=[agout0.ap().opt()]))

    pf = {}
    pf[(0, 'q')] = wqkv.load(wq_r[:, :, 0:256])
    pf[(0, 'k')] = wqkv.load(wk_r[:, :, 0:256])
    pf[(0, 'v')] = wqkv.load(wv_r[:, :, 0:256])
    pf[(0, 't')] = tbs.load(tbl_r[0:2, :, :].rearrange("h p c -> p h c"))

    k.wait(GP, dv.evt())
    e_st1 = agsem.mark(GP.dma_start(out=agin1[:, :], in_=Sst[:, 8 * HD:]))
    k.wait(GP, e_st1)
    e_cc1 = ccs.mark(GP.collective_compute(
        "AllGather", ALU.bypass,
        replica_groups=[[0, 1, 2, 3], [4, 5, 6, 7]],
        ins=[agin1.ap().opt()], outs=[agout1.ap().opt()]))
    def combine_half(half, e_cc, agout_h, gb_tensor):
        gev = None
        k.wait(SY, e_cc)
        for jj in range(4):
            k.wait(SY, gev)
            e_g = gsem[half].mark(SY.dma_start(out=gb_tensor[:, :],
                                               in_=agout_h[jj * 128:(jj + 1) * 128, :]))
            k.wait(DV, e_g)
            k.wait(DV, e_cf)
            k.wait(DV, agsem.evt())
            sdv()
            last = None
            for hh in range(8):
                h = half * 8 + hh
                col = cf_s[:, jj * NH + h: jj * NH + h + 1]
                gb = gb_tensor[:, hh * HD:(hh + 1) * HD]
                if jj == 0:
                    last = DV.tensor_scalar_mul(Sst[:, h * HD:(h + 1) * HD], gb, col)
                else:
                    last = DV.scalar_tensor_tensor(
                        Sst[:, h * HD:(h + 1) * HD], gb, col,
                        Sst[:, h * HD:(h + 1) * HD], ALU.mult, ALU.add)
            gev = dv.mark(last)

    combine_half(0, e_cc0, agout0, osq_scr)

    if debug:
        k.wait(SY, dv.evt())
        _e = sto.mark(SY.dma_start(out=dbg["d_S"][:, :], in_=Sst[:, :]))
        for _eng in (TE, DV, AC):
            k.wait(_eng, _e)

    with big_arena as arena:
        with ExitStack() as sub1:
            qfm = k.sb("qfm", [128, 2 * TOK], BF16, es=sub1)
            kfm = k.sb("kfm", [128, 2 * TOK], BF16, es=sub1)
            vtm = k.sb("vtma", [128, TC * BLK], BF16, es=sub1)
            kptm = k.sb("kptma", [128, TC * BLK], BF16, es=sub1)


            prev_ret = None
            for g in range(8):
                if g == 2:
                    combine_half(1, e_cc1, agout1, g_scr)
                h0 = 2 * g
                sq, eq_ = pf.pop((g, 'q'))
                sk, ek_ = pf.pop((g, 'k'))
                stb, etb = pf.pop((g, 't'))

                for (ss, ee, dst) in [(sq, eq_, qfm), (sk, ek_, kfm)]:
                    last_ev = None
                    for hh in range(2):
                        for th in range(2):
                            b, bank = pool.get(TE)
                            k.wait(TE, ee)
                            k.wait(TE, e_hfm)
                            for cc in range(CC):
                                ins = TE.matmul(bank[:, 0:512],
                                          wqkv.bufs[ss][:, cc * 256 + hh * 128: cc * 256 + hh * 128 + 128],
                                          hfm[:, cc * TOK + th * 512: cc * TOK + th * 512 + 512],
                                          start=(cc == 0), stop=(cc == CC - 1))
                            e_mm = te.mark(ins)
                            k.wait(AC, e_mm)
                            last_ev = ac.mark(AC.activation(
                                dst[:, hh * TOK + th * 512: hh * TOK + th * 512 + 512],
                                bank[:, 0:512], AF.Silu))
                            pool.put(b, last_ev)
                    wqkv.done(ss, last_ev)
                    if ss == sq and g + 1 < 8:
                        pf[(g + 1, 'q')] = wqkv.load(wq_r[:, :, (h0 + 2) * 128:(h0 + 4) * 128])
                        pf[(g + 1, 't')] = tbs.load(tbl_r[h0 + 2:h0 + 4, :, :].rearrange("h p c -> p h c"))
                    elif ss == sk and g + 1 < 8:
                        pf[(g + 1, 'k')] = wqkv.load(wk_r[:, :, (h0 + 2) * 128:(h0 + 4) * 128])
                e_qk = ac.evt()

                sv, ev_ = pf.pop((g, 'v'))
                last_ev = None
                for tsub in range(TC):
                    b, bank = pool.get(TE)
                    k.wait(TE, ev_)
                    k.wait(TE, e_hfm)
                    for cc in range(CC):
                        ins = TE.matmul(bank[:, 0:BLK],
                                  hfm[:, cc * TOK + tsub * 128: cc * TOK + tsub * 128 + 128],
                                  wqkv.bufs[sv][:, cc * BLK:(cc + 1) * BLK],
                                  start=(cc == 0), stop=(cc == CC - 1))
                    e_mm = te.mark(ins)
                    k.wait(AC, e_mm)
                    last_ev = ac.mark(AC.activation(
                        vtm[:, tsub * BLK:(tsub + 1) * BLK], bank[:, 0:BLK], AF.Silu))
                    pool.put(b, last_ev)
                wqkv.done(sv, last_ev)
                if g + 1 < 8:
                    pf[(g + 1, 'v')] = wqkv.load(wv_r[:, :, (h0 + 2) * 128:(h0 + 4) * 128])
                e_v = ac.evt()

                for tsub in range(TC):
                    ti = tb_i[0] % 2
                    tb_i[0] += 1
                    tb = tbs_ps[ti]
                    k.wait(TE, tb_evt[ti])
                    k.wait(TE, e_qk)
                    for hh in range(2):
                        ins = TE.transpose(tb[:, hh * 128:(hh + 1) * 128],
                                           kfm[:, hh * TOK + tsub * 128: hh * TOK + tsub * 128 + 128],
                                           idm_s[:, :])
                    e_t = te.mark(ins)
                    k.wait(DV, e_t)
                    k.wait(DV, e_kdc)
                    last = None
                    for hh in range(2):
                        h = h0 + hh
                        last = DV.tensor_scalar_mul(
                            kptm[:, tsub * BLK + hh * 128: tsub * BLK + hh * 128 + 128],
                            tb[:, hh * 128:(hh + 1) * 128],
                            kdc_s[:, 2 * h + (tsub % 2): 2 * h + (tsub % 2) + 1])
                    e_kp = dv.mark(last)
                    tb_evt[ti] = e_kp
                e_kp = dv.evt()

                tbuf = tbs.bufs[stb]
                iters = [(hh, blk) for hh in range(2) for blk in range(NB)]
                c1ev = [None] * len(iters)
                qqev = [None] * len(iters)
                c1bank = [None] * len(iters)

                def stage_a(i):
                    hh, blk = iters[i]
                    par = i % 2
                    k.wait(DV, etb)
                    k.wait(DV, bprev[0])
                    qqev[i] = dv.mark(DV.tensor_tensor(
                        qq_scr[:, par * BLK:(par + 1) * BLK],
                        qfm[:, hh * TOK + blk * BLK: hh * TOK + (blk + 1) * BLK],
                        tbuf[:, hh * 768: hh * 768 + 256], ALU.mult))
                    b1, bank1 = pool.get(TE)
                    k.wait(TE, e_qk)
                    for nsub in range(2):
                        ins = TE.matmul(bank1[:, nsub * 256:(nsub + 1) * 256],
                                  kfm[:, hh * TOK + blk * BLK + nsub * 128:
                                      hh * TOK + blk * BLK + nsub * 128 + 128],
                                  qfm[:, hh * TOK + blk * BLK: hh * TOK + (blk + 1) * BLK],
                                  start=True, stop=True)
                    e_c1 = te.mark(ins)
                    k.wait(DV, e_c1)
                    last = None
                    for nsub in range(2):
                        last = DV.tensor_tensor(
                            c1_scr[:, par * 512 + nsub * 256: par * 512 + (nsub + 1) * 256],
                            bank1[:, nsub * 256:(nsub + 1) * 256],
                            tbuf[:, hh * 768 + 256 + nsub * 256:
                                 hh * 768 + 256 + (nsub + 1) * 256],
                            ALU.mult)
                    c1ev[i] = dv.mark(last)
                    pool.put(b1, c1ev[i])

                bprev = [prev_ret]

                sbfev = [None] * len(iters)

                def stage_b_pre(i):
                    hh, blk = iters[i]
                    par = i % 2
                    h = h0 + hh
                    sdv()
                    sbfev[i] = dv.mark(DV.tensor_copy(
                        sbf_scr[:, par * HD:(par + 1) * HD], Sst[:, h * HD:(h + 1) * HD]))

                def stage_b(i):
                    hh, blk = iters[i]
                    par = i % 2
                    h = h0 + hh
                    e_sbf = sbfev[i]
                    b2, bank2 = pool.get(TE)
                    k.wait(TE, c1ev[i])
                    k.wait(TE, e_sbf)
                    k.wait(TE, qqev[i])
                    k.wait(TE, e_v)
                    for nsub in range(2):
                        tsub = blk * 2 + nsub
                        TE.matmul(bank2[:, 0:256],
                                  vtm[:, tsub * BLK + hh * 128: tsub * BLK + hh * 128 + 128],
                                  c1_scr[:, par * 512 + nsub * 256: par * 512 + (nsub + 1) * 256],
                                  start=(nsub == 0), stop=False)
                    ins = TE.matmul(bank2[:, 0:256], sbf_scr[:, par * HD:(par + 1) * HD],
                                    qq_scr[:, par * BLK:(par + 1) * BLK],
                                    start=False, stop=True)
                    e_o = te.mark(ins)
                    bprev[0] = e_o
                    k.wait(DV, e_o)
                    e_oe = dv.mark(DV.tensor_copy(
                        obf[:, h * TOK + blk * BLK: h * TOK + (blk + 1) * BLK],
                        bank2[:, 0:256]))
                    pool.put(b2, e_oe)
                    b3, bank3 = pool.get(TE)
                    k.wait(TE, e_kp)
                    for nsub in range(2):
                        tsub = blk * 2 + nsub
                        ins = TE.matmul(bank3[:, 0:128],
                                  kptm[:, tsub * BLK + hh * 128: tsub * BLK + hh * 128 + 128],
                                  vtm[:, tsub * BLK + hh * 128: tsub * BLK + hh * 128 + 128],
                                  start=(nsub == 0), stop=(nsub == 1))
                    e_cb = te.mark(ins)
                    k.wait(DV, e_cb)
                    sdv()
                    e_su = dv.mark(DV.scalar_tensor_tensor(
                        Sst[:, h * HD:(h + 1) * HD], Sst[:, h * HD:(h + 1) * HD],
                        bdk[h], bank3[:, 0:128], ALU.mult, ALU.add))
                    pool.put(b3, e_su)

                stage_a(0)
                for i in range(len(iters)):
                    stage_b_pre(i)
                    if i + 1 < len(iters):
                        stage_a(i + 1)
                    stage_b(i)
                prev_ret = bprev[0]
                tbs.done(stb, dv.evt())
            e_obf = dv.evt()

            if debug:
                k.wait(SY, e_obf)
                sto.mark(SY.dma_start(out=dbg["d_hfm"][:, :], in_=hfm[:, :]))
                _e = sto.mark(SY.dma_start(out=dbg["d_obf"][:, :], in_=obf[:, :]))
                for _eng in (TE, DV, AC):
                    k.wait(_eng, _e)

            br0, bankr0 = pool.reserve(TE)
            br1, bankr1 = pool.reserve(TE)
            e_sqprev = None
            e_gprev = None
            pg = wqkv.load(wga_r[:, :, 0:256])
            for h in range(NH):
                if h % 2 == 0:
                    sg, eg_ = pg
                    if h + 2 < NH:
                        pg = wqkv.load(wga_r[:, :, (h + 2) * 128:(h + 4) * 128])
                hh = h % 2
                k.wait(DV, e_obf)
                k.wait(DV, e_sqprev)
                sdv()
                e_sq = dv.mark(DV.tensor_tensor(
                    osq_scr[:, :], obf[:, h * TOK:(h + 1) * TOK],
                    obf[:, h * TOK:(h + 1) * TOK], ALU.mult))
                k.wait(TE, e_sq)
                k.wait(TE, e_onc)
                TE.matmul(bankr0[0:1, 0:512], onc_s[:, :], osq_scr[:, 0:512],
                          start=(h == 0), stop=(h == NH - 1))
                ins = TE.matmul(bankr1[0:1, 0:512], onc_s[:, :], osq_scr[:, 512:1024],
                                start=(h == 0), stop=(h == NH - 1))
                e_sqprev = te.mark(ins)
                for th in range(2):
                    b, bank = pool.get(TE)
                    k.wait(TE, eg_)
                    k.wait(TE, e_hfm)
                    for cc in range(CC):
                        ins = TE.matmul(bank[:, 0:512],
                                  wqkv.bufs[sg][:, cc * 256 + hh * 128: cc * 256 + hh * 128 + 128],
                                  hfm[:, cc * TOK + th * 512: cc * TOK + th * 512 + 512],
                                  start=(cc == 0), stop=(cc == CC - 1))
                    e_mm = te.mark(ins)
                    k.wait(AC, e_mm)
                    k.wait(AC, e_gprev)
                    e_g = ac.mark(AC.activation(g_scr[:, th * 512:(th + 1) * 512],
                                                bank[:, 0:512], AF.Sigmoid))
                    pool.put(b, e_g)
                if hh == 1:
                    wqkv.done(sg, ac.evt())
                k.wait(DV, ac.evt())
                k.wait(DV, e_nwc)
                sdv()
                e_gprev = dv.mark(DV.scalar_tensor_tensor(
                    obf[:, h * TOK:(h + 1) * TOK], obf[:, h * TOK:(h + 1) * TOK],
                    nwc_s[:, h:h + 1], g_scr[:, :], ALU.mult, ALU.mult))
            e_ofin = dv.evt()
            k.wait(DV, e_sqprev)
            dv.mark(DV.tensor_scalar_mul(r_row[0:1, 0:512], bankr0[0:1, 0:512], 1.0 / D))
            dv.mark(DV.tensor_scalar_mul(r_row[0:1, 512:1024], bankr1[0:1, 0:512], 1.0 / D))
            sdv()
            e_rr = dv.mark(DV.tensor_scalar_add(r_row[0:1, :], r_row[0:1, :], RMS_EPS))
            pool.put(br0, e_rr)
            pool.put(br1, e_rr)
            bb, bankb = pool.get(TE)
            k.wait(TE, e_rr)
            k.wait(TE, e_onr)
            for tsub in range(TC):
                ins = TE.matmul(bankb[:, tsub:tsub + 1],
                                r_row[0:1, tsub * 128:(tsub + 1) * 128],
                                onr_s[0:1, 0:1], start=True, stop=True)
            e_b = te.mark(ins)
            k.wait(DV, e_b)
            e_vc = dv.mark(DV.tensor_copy(rcol2_s[:, 0:TC], bankb[:, 0:TC]))
            pool.put(bb, e_vc)
            k.wait(AC, e_vc)
            ac.mark(AC.activation(rcol_s[:, 0:TC], rcol2_s[:, 0:TC], AF.Ln))
            sac()
            e_rc = ac.mark(AC.activation(rcol2_s[:, 0:TC], rcol_s[:, 0:TC], AF.Exp, scale=-0.5))

        wos = Stream(k, "wos", CC * 256, es=arena, initial_free=snap())
        for cs in range(8):
            so, eo_ = wos.load(wout_r[:, :, cs * 256:(cs + 1) * 256])
            for tsub in range(TC):
                b, bank = pool.get(TE)
                k.wait(TE, eo_)
                k.wait(TE, e_ofin)
                for hc in range(NH):
                    ins = TE.matmul(bank[:, 0:256],
                              obf[:, hc * TOK + tsub * 128: hc * TOK + tsub * 128 + 128],
                              wos.bufs[so][:, hc * 256:(hc + 1) * 256],
                              start=(hc == 0), stop=(hc == NH - 1))
                e_mm = te.mark(ins)
                k.wait(DV, e_mm)
                k.wait(DV, e_rc)
                sdv()
                e_acc = dv.mark(DV.scalar_tensor_tensor(
                    xres[:, tsub * D + cs * 256: tsub * D + cs * 256 + 256],
                    bank[:, 0:256], rcol2_s[:, tsub:tsub + 1],
                    xres[:, tsub * D + cs * 256: tsub * D + cs * 256 + 256],
                    ALU.mult, ALU.add))
                pool.put(b, e_acc)
            wos.done(so, dv.evt())
    e_x1 = dv.evt()

    if debug:
        k.wait(SY, e_x1)
        _e = sto.mark(SY.dma_start(out=dbg["d_x1"][:, :], in_=xres[:, :]))
        for _eng in (TE, DV, AC):
            k.wait(_eng, _e)

    with ExitStack() as arena:
        hf32_scr = k.sb("hf32_scr", [128, D], F32, es=arena)
        hlotm_scr = k.sb("hlotm", [128, D], BF16, es=arena)
        wgf_s = k.sb("wgf_s", [128, CC * E], F32, es=arena)
        whi = k.sb("whi", [128, CC * E], BF16, es=arena)
        wlo = k.sb("wlo", [128, CC * E], BF16, es=arena)

        k.wait(SY, snap())
        wgfsem = Ctr(k, "wgfsem", 16)
        e_wgf = wgfsem.mark(SY.dma_start(out=wgf_s[:, :], in_=wgf_r[:, :, :]))

        with ExitStack() as sub1:
            hlo = k.sb("hlo", [128, CC * TOK], BF16, es=sub1)
            hlo_v = hlo.ap().rearrange("p (cc t) -> p cc t", t=TOK)

            for tc in range(TC):
                k.wait(DV, e_x1)
                e_h = ln_chunk(xres[:, tc * D:(tc + 1) * D], htm_scr[:, :],
                               guard=te.evt(), f32_ap=hf32_scr[:, :])
                sdv()
                e_lo = dv.mark(DV.tensor_tensor(hlotm_scr[:, :], hf32_scr[:, :],
                                                htm_scr[:, :], ALU.subtract))
                transpose16(htm_scr, hfm_v, tc * 128, e_h)
                transpose16(hlotm_scr, hlo_v, tc * 128, e_lo)
            e_hfm2 = dv.evt()

            k.wait(DV, e_wgf)
            e_whi = dv.mark(DV.tensor_copy(whi[:, :], wgf_s[:, :]))
            sdv()
            e_wlo = dv.mark(DV.tensor_tensor(wlo[:, :], wgf_s[:, :], whi[:, :], ALU.subtract))

            k.wait(DV, ac.evt())
            for tsub in range(TC):
                b, bank = pool.get(TE)
                k.wait(TE, e_whi)
                k.wait(TE, e_wlo)
                k.wait(TE, e_hfm2)
                first = True
                for pa, (lh, rh) in enumerate([(hfm, whi), (hfm, wlo), (hlo, whi)]):
                    for cc in range(CC):
                        ins = TE.matmul(bank[:, 0:E],
                                  lh[:, cc * TOK + tsub * 128: cc * TOK + tsub * 128 + 128],
                                  rh[:, cc * E:(cc + 1) * E],
                                  start=first, stop=(pa == 2 and cc == CC - 1))
                        first = False
                e_lg = te.mark(ins)
                L = bank[:, 0:E]
                k.wait(DV, e_lg)
                sdv()
                dv.mark(DV.reduce_max(stats[:, 0:1], L, axis=AX.X))
                sdv()
                dv.mark(DV.tensor_scalar(rsc[:, 0:E], L, stats[:, 0:1], None, ALU.is_equal))
                sdv()
                dv.mark(DV.scalar_tensor_tensor(rsc[:, 8:16], rsc[:, 0:E], -1e30, L,
                                                ALU.mult, ALU.add))
                sdv()
                dv.mark(DV.reduce_max(stats[:, 1:2], rsc[:, 8:16], axis=AX.X))
                sdv()
                dv.mark(DV.tensor_scalar(rsc[:, 0:E], L, stats[:, 1:2], None, ALU.is_ge))
                dv.mark(DV.tensor_scalar_mul(stats[:, 2:3], stats[:, 0:1], -1.0))
                sdv()
                e_nm1 = dv.mark(DV.tensor_tensor(stats[:, 3:4], stats[:, 1:2],
                                                 stats[:, 0:1], ALU.subtract))
                k.wait(AC, e_nm1)
                ac.mark(AC.activation(rsc[:, 16:24], L, AF.Exp, bias=stats[:, 2:3]))
                e_e2 = ac.mark(AC.activation(stats[:, 4:5], stats[:, 3:4], AF.Exp))
                k.wait(DV, e_e2)
                dv.mark(DV.tensor_scalar_add(stats[:, 5:6], stats[:, 4:5], 1.0))
                sdv()
                dv.mark(DV.reciprocal(stats[:, 6:7], stats[:, 5:6]))
                dv.mark(DV.tensor_tensor(rsc[:, 24:32], rsc[:, 16:24], rsc[:, 0:E], ALU.mult))
                sdv()
                e_w = dv.mark(DV.tensor_scalar_mul(wrt[:, tsub * E:(tsub + 1) * E],
                                                   rsc[:, 24:32], stats[:, 6:7]))
                pool.put(b, e_w)
            e_wrt = dv.evt()

            if debug:
                k.wait(SY, e_wrt)
                _e = sto.mark(SY.dma_start(out=dbg["d_wrt"][:, :], in_=wrt[:, :]))
                for _eng in (TE, DV, AC):
                    k.wait(_eng, _e)

        h1 = [k.sb(f"h1_{i}", [128, 8 * TOK], BF16, es=arena) for i in range(2)]
        w1s = Stream(k, "w1s", CC * 128, es=arena, initial_free=snap())
        w2s = Stream(k, "w2s", 8 * 512, es=arena, initial_free=snap())
        h1ev = [None, None]

        tasks = [(e, fs) for e in range(E + 1) for fs in range(4)]
        s1done = {}

        def emit_s1(i):
            e, fs = tasks[i]
            w1src = ws1_r if e == E else w1_r[e]
            sl = i % 2
            hb = h1[sl]
            last_s1 = None
            for fc in range(8):
                s1, e1_ = w1s.load(w1src[:, :, fs * 1024 + fc * 128: fs * 1024 + fc * 128 + 128])
                for th in range(2):
                    b, bank = pool.get(TE)
                    k.wait(TE, e1_)
                    k.wait(TE, e_hfm2)
                    k.wait(TE, h1ev[sl])
                    for cc in range(CC):
                        ins = TE.matmul(bank[:, 0:512],
                                  w1s.bufs[s1][:, cc * 128:(cc + 1) * 128],
                                  hfm[:, cc * TOK + th * 512: cc * TOK + th * 512 + 512],
                                  start=(cc == 0), stop=(cc == CC - 1))
                    e_mm = te.mark(ins)
                    k.wait(AC, e_mm)
                    last_s1 = ac.mark(AC.activation(
                        hb[:, fc * TOK + th * 512: fc * TOK + th * 512 + 512],
                        bank[:, 0:512], AF.Silu))
                    pool.put(b, last_s1)
                w1s.done(s1, last_s1)
            s1done[i] = last_s1

        def emit_s2(i):
            e, fs = tasks[i]
            w2src = ws2_r if e == E else w2_r[e]
            shared = (e == E)
            sl = i % 2
            hb = h1[sl]
            e_h1 = s1done[i]
            last_te = None
            for cs in range(4):
                s2, e2_ = w2s.load(w2src[:, fs * 8:(fs + 1) * 8, cs * 512:(cs + 1) * 512])
                for tsub in range(TC):
                    b, bank = pool.get(TE)
                    k.wait(TE, e2_)
                    k.wait(TE, e_h1)
                    for fc in range(8):
                        ins = TE.matmul(bank[:, 0:512],
                                  hb[:, fc * TOK + tsub * 128: fc * TOK + tsub * 128 + 128],
                                  w2s.bufs[s2][:, fc * 512:(fc + 1) * 512],
                                  start=(fc == 0), stop=(fc == 7))
                    e_mm = te.mark(ins)
                    last_te = e_mm
                    k.wait(DV, e_mm)
                    xr = xres[:, tsub * D + cs * 512: tsub * D + cs * 512 + 512]
                    sdv()
                    if shared:
                        e_acc = dv.mark(DV.tensor_tensor(xr, xr, bank[:, 0:512], ALU.add))
                    else:
                        k.wait(DV, e_wrt)
                        e_acc = dv.mark(DV.scalar_tensor_tensor(
                            xr, bank[:, 0:512], wrt[:, tsub * E + e: tsub * E + e + 1],
                            xr, ALU.mult, ALU.add))
                    pool.put(b, e_acc)
                w2s.done(s2, dv.evt())
            h1ev[sl] = last_te

        emit_s1(0)
        for i in range(len(tasks)):
            if i + 1 < len(tasks):
                emit_s1(i + 1)
            emit_s2(i)

    k.wait(SY, dv.evt())
    e_out = sto.mark(SY.dma_start(out=out_r[:, :, :], in_=xres_v[:, :, :]))
    k.wait(SY, e_out)

    k.es.close()
    return nc



def _host_inputs(L0_attn_x, Wqkv, Wg, Wout, norm_w, fWg, W1, W2, Ws1, Ws2):
    x = np.ascontiguousarray(np.asarray(L0_attn_x, dtype=np.float32).reshape(2 * 4096, D))
    Wqkv = np.asarray(Wqkv, dtype=np.float32).reshape(D, NH, 3, HD)
    wq = np.ascontiguousarray(Wqkv[:, :, 0, :].reshape(D, D))
    wk = np.ascontiguousarray(Wqkv[:, :, 1, :].reshape(D, D))
    wv = np.ascontiguousarray(Wqkv[:, :, 2, :].reshape(D, D))

    slopes = np.array(_slopes(), dtype=np.float64)
    m = np.arange(BLK, dtype=np.float64)
    qd = np.exp(-slopes[:, None] * (m[None, :] + 1.0))
    kd = np.exp(-slopes[:, None] * (BLK - 1.0 - m[None, :]))
    idx = m[None, :, None] - m[None, None, :]
    dd = np.where(idx >= 0, np.exp(-slopes[:, None, None] * idx), 0.0)
    ddT = np.ascontiguousarray(np.swapaxes(dd, 1, 2))

    tbl = np.zeros((NH, 128, 768), dtype=np.float32)
    tbl[:, :, 0:256] = qd[:, None, :]
    tbl[:, :, 256:512] = ddT[:, 0:128, :]
    tbl[:, :, 512:768] = ddT[:, 128:256, :]

    kdc = np.zeros((128, 2 * NH), dtype=np.float32)
    for h in range(NH):
        kdc[:, 2 * h] = kd[h, 0:128]
        kdc[:, 2 * h + 1] = kd[h, 128:256]

    nwc = np.ascontiguousarray(np.asarray(norm_w, dtype=np.float32).reshape(NH, 128).T)

    shared = dict(
        wq=wq, wk=wk, wv=wv,
        wga=np.ascontiguousarray(np.asarray(Wg, dtype=np.float32)),
        wout=np.ascontiguousarray(np.asarray(Wout, dtype=np.float32)),
        wgf=np.ascontiguousarray(np.asarray(fWg, dtype=np.float32)),
        w1=np.ascontiguousarray(np.asarray(W1, dtype=np.float32)),
        w2=np.ascontiguousarray(np.asarray(W2, dtype=np.float32)),
        ws1=np.ascontiguousarray(np.asarray(Ws1, dtype=np.float32)),
        ws2=np.ascontiguousarray(np.asarray(Ws2, dtype=np.float32)),
        tbl=tbl, kdc=kdc, nwc=nwc,
        idm=np.eye(128, dtype=np.float32),
        onc=np.ones((128, 1), dtype=np.float32),
        onr=np.ones((1, 128), dtype=np.float32),
    )
    bd = np.exp(-slopes * BLK)
    in_maps = []
    for c in range(8):
        b, j = c // 4, c % 4
        row0 = b * 4096
        xo = np.ascontiguousarray(x[row0 + j * TOK: row0 + (j + 1) * TOK])
        cf = np.zeros((128, 64), dtype=np.float32)
        for jj in range(j):
            cf[:, jj * NH:(jj + 1) * NH] = (bd ** (NB * (j - 1 - jj)))[None, :]
        in_maps.append(dict(shared, xo=xo, cf=cf))
    return in_maps


_NC_CACHE = {}


def kernel(L0_attn_x, L0_attn_Wqkv, L0_attn_Wg, L0_attn_Wout, L0_attn_norm_w,
           L0_ffn_Wg, L0_ffn_W1, L0_ffn_W2, L0_ffn_Ws1, L0_ffn_Ws2):
    in_maps = _host_inputs(L0_attn_x, L0_attn_Wqkv, L0_attn_Wg, L0_attn_Wout,
                           L0_attn_norm_w, L0_ffn_Wg, L0_ffn_W1, L0_ffn_W2,
                           L0_ffn_Ws1, L0_ffn_Ws2)
    if "nc" not in _NC_CACHE:
        _NC_CACHE["nc"] = build_nc(debug=False)
    nc = _NC_CACHE["nc"]
    res = run_bass_kernel_spmd(nc, in_maps, core_ids=list(range(8)))
    shards = [np.asarray(r["out"]) for r in res.results]
    full = np.concatenate(shards, axis=0).reshape(2, 4096, D)
    return full.astype(np.float32)


# version 14
# speedup vs baseline: 1.0118x; 1.0004x over previous
import math
import numpy as np
from contextlib import ExitStack

import concourse.bass as bass
import concourse.mybir as mybir
from concourse.bass_utils import run_bass_kernel_spmd

F32 = mybir.dt.float32
BF16 = mybir.dt.bfloat16
AF = mybir.ActivationFunctionType
ALU = mybir.AluOpType
AX = mybir.AxisListType

D = 2048
CC = 16
TOK = 1024
TC = 8
NH = 16
HD = 128
BLK = 256
NB = 4
PRE = 3072
FF = 4096
E = 8
LN_EPS = 1e-5
RMS_EPS = 1e-6


def _slopes():
    start = 2.0 ** (-(2.0 ** -(math.log2(NH) - 3)))
    return [start * (start ** i) for i in range(NH)]


class Ctr:

    def __init__(self, k, name, step=1):
        self.sem = k.es.enter_context(k.nc.semaphore(name))
        self.val = 0
        self.step = step

    def mark(self, instr):
        instr.then_inc(self.sem, self.step)
        self.val += self.step
        return (self.sem, self.val)

    def evt(self):
        return (self.sem, self.val)


class Kb:
    def __init__(self):
        self.nc = bass.Bass("TRN2", target_bir_lowering=False, debug=False)
        self.es = ExitStack()
        self._waited = {}

    def sb(self, name, shape, dtype, es=None):
        return (es or self.es).enter_context(self.nc.sbuf_tensor(name, shape, dtype))

    def ps(self, name, shape, dtype):
        return self.es.enter_context(self.nc.psum_tensor(name, shape, dtype))

    def wait(self, eng, evt):
        if evt is None:
            return
        if isinstance(evt, (list, tuple)) and evt and isinstance(evt[0], (list, tuple)):
            for e in evt:
                self.wait(eng, e)
            return
        sem, val = evt
        if val <= 0:
            return
        key = (id(eng), id(sem))
        if self._waited.get(key, -1) >= val:
            return
        eng.wait_ge(sem, val)
        self._waited[key] = val


class Pool:

    def __init__(self, k, n=7):
        self.k = k
        self.banks = [k.ps(f"pb{i}", [128, 512], F32) for i in range(n)]
        self.evts = [None] * n
        self.i = 0
        self.n = n
        self.skip = set()

    def get(self, eng):
        while True:
            b = self.i % self.n
            self.i += 1
            if b not in self.skip:
                break
        self.k.wait(eng, self.evts[b])
        return b, self.banks[b]

    def reserve(self, eng):
        b, bank = self.get(eng)
        self.skip.add(b)
        return b, bank

    def put(self, b, evt):
        self.skip.discard(b)
        self.evts[b] = evt


class Stream:

    def __init__(self, k, name, cols, dtype=BF16, nbuf=2, es=None, initial_free=None):
        self.k = k
        self.bufs = [k.sb(f"{name}{i}", [128, cols], dtype, es=es) for i in range(nbuf)]
        self.sems = [k.es.enter_context(k.nc.semaphore(f"{name}s{i}")) for i in range(nbuf)]
        self.cnt = [0] * nbuf
        self.free = [list(initial_free) if initial_free else None for _ in range(nbuf)]
        self.i = 0

    def load(self, src_ap, cols=None):
        s = self.i % len(self.bufs)
        self.i += 1
        self.k.wait(self.k.nc.gpsimd, self.free[s])
        dst = self.bufs[s][:, :cols] if cols else self.bufs[s][:, :]
        ins = self.k.nc.gpsimd.dma_start(out=dst, in_=src_ap)
        ins.then_inc(self.sems[s], 16)
        self.cnt[s] += 16
        return s, (self.sems[s], self.cnt[s])

    def done(self, s, evt):
        self.free[s] = evt


def build_nc(debug=False):
    k = Kb()
    nc = k.nc
    TE, DV, AC, SY, GP = nc.tensor, nc.vector, nc.scalar, nc.sync, nc.gpsimd

    din = {}
    for name, shape in [
        ("xo", [TOK, D]),
        ("wq", [D, D]), ("wk", [D, D]), ("wv", [D, D]),
        ("wga", [D, D]), ("wout", [D, D]), ("wgf", [D, E]),
        ("w1", [E, D, FF]), ("w2", [E, FF, D]),
        ("ws1", [D, FF]), ("ws2", [FF, D]),
        ("tbl", [NH, 128, 768]),
        ("kdc", [128, 2 * NH]), ("nwc", [128, NH]), ("cf", [128, 64]),
        ("idm", [128, 128]), ("onc", [128, 1]), ("onr", [1, 128]),
    ]:
        din[name] = nc.dram_tensor(name, shape, F32, kind="ExternalInput").ap()
    out = nc.dram_tensor("out", [TOK, D], F32, kind="ExternalOutput").ap()
    agin0 = nc.dram_tensor("agin0", [128, 8 * HD], BF16)
    agout0 = nc.dram_tensor("agout0", [4 * 128, 8 * HD], BF16)
    agin1 = nc.dram_tensor("agin1", [128, 8 * HD], BF16)
    agout1 = nc.dram_tensor("agout1", [4 * 128, 8 * HD], BF16)

    dbg = {}
    if debug:
        for name, shape, dt in [
            ("d_hfm", [128, CC * TOK], BF16), ("d_obf", [128, NH * TOK], BF16),
            ("d_S", [128, NH * HD], F32), ("d_x1", [128, TC * D], F32),
            ("d_wrt", [128, 64], F32), ("d_r", [1, TOK], F32),
        ]:
            dbg[name] = nc.dram_tensor(name, shape, dt, kind="ExternalOutput").ap()

    xo_r = din["xo"].rearrange("(tc p) c -> p tc c", p=128)
    out_r = out.rearrange("(tc p) c -> p tc c", p=128)
    wq_r = din["wq"].rearrange("(cc p) n -> p cc n", p=128)
    wk_r = din["wk"].rearrange("(cc p) n -> p cc n", p=128)
    wv_r = din["wv"].rearrange("(cc p) n -> p cc n", p=128)
    wga_r = din["wga"].rearrange("(cc p) n -> p cc n", p=128)
    wout_r = din["wout"].rearrange("(hc p) n -> p hc n", p=128)
    wgf_r = din["wgf"].rearrange("(cc p) n -> p cc n", p=128)
    w1_r = din["w1"].rearrange("e (cc p) f -> e p cc f", p=128)
    w2_r = din["w2"].rearrange("e (fc p) c -> e p fc c", p=128)
    ws1_r = din["ws1"].rearrange("(cc p) f -> p cc f", p=128)
    ws2_r = din["ws2"].rearrange("(fc p) c -> p fc c", p=128)
    tbl_r = din["tbl"]

    slopes = _slopes()
    bdk = [math.exp(-s * BLK) for s in slopes]

    xres = k.sb("xres", [128, TC * D], F32)
    hfm = k.sb("hfm", [128, CC * TOK], BF16)
    Sst = k.sb("Sst", [128, NH * HD], F32)
    kdc_s = k.sb("kdc_s", [128, 2 * NH], F32)
    nwc_s = k.sb("nwc_s", [128, NH], F32)
    cf_s = k.sb("cf_s", [128, 64], F32)
    idm_s = k.sb("idm_s", [128, 128], BF16)
    onc_s = k.sb("onc_s", [128, 1], BF16)
    onr_s = k.sb("onr_s", [1, 128], F32)
    stats = k.sb("stats", [128, 8], F32)
    rsc = k.sb("rsc", [128, 32], F32)
    r_row = k.sb("r_row", [1, TOK], F32)
    rcol_s = k.sb("rcol_s", [128, TC], F32)
    rcol2_s = k.sb("rcol2_s", [128, TC], F32)
    wrt = k.sb("wrt", [128, 64], F32)
    qq_scr = k.sb("qq_scr", [128, 2 * BLK], BF16)
    c1_scr = k.sb("c1_scr", [128, 4 * BLK], BF16)
    sbf_scr = k.sb("sbf_scr", [128, 2 * HD], BF16)
    osq_scr = k.sb("osq_scr", [128, TOK], BF16)
    g_scr = k.sb("g_scr", [128, TOK], BF16)
    htm_scr = k.sb("htm_scr", [128, D], BF16)

    pool = Pool(k, n=6)
    tbs_ps = [k.ps(f"tb{i}", [128, 1024], BF16) for i in range(2)]
    tb_evt = [None, None]
    tb_i = [0]

    te = Ctr(k, "te")
    dv = Ctr(k, "dv")
    ac = Ctr(k, "ac")
    syl = Ctr(k, "syl", 16)
    gpl = Ctr(k, "gpl", 16)
    sto = Ctr(k, "sto", 16)

    def snap():
        return [te.evt(), dv.evt(), ac.evt()]

    xres_v = xres.ap().rearrange("p (tc c) -> p tc c", c=D)

    SY.dma_start(out=xres_v[:, :, :], in_=xo_r[:, :, :]).then_inc(syl.sem, 16)
    SY.dma_start(out=kdc_s[:, :], in_=din["kdc"][:, :]).then_inc(syl.sem, 16)
    SY.dma_start(out=nwc_s[:, :], in_=din["nwc"][:, :]).then_inc(syl.sem, 16)
    SY.dma_start(out=onr_s[:, :], in_=din["onr"][:, :]).then_inc(syl.sem, 16)
    SY.dma_start(out=cf_s[:, :], in_=din["cf"][:, :]).then_inc(syl.sem, 16)
    syl.val += 80
    e_xres0 = e_kdc = e_nwc = e_onr = e_cf = syl.evt()
    GP.dma_start(out=idm_s[:, :], in_=din["idm"][:, :]).then_inc(gpl.sem, 16)
    GP.dma_start(out=onc_s[:, :], in_=din["onc"][:, :]).then_inc(gpl.sem, 16)
    gpl.val += 32
    e_idm = e_onc = gpl.evt()
    e_S0 = dv.mark(DV.memset(Sst[:, :], 0.0))

    def sdv():
        k.wait(DV, dv.evt())

    def sac():
        k.wait(AC, ac.evt())

    def dvo(instr):
        return dv.mark(instr)

    def ln_chunk(x_ap, out_bf_ap, guard=None, f32_ap=None):
        k.wait(DV, guard)
        for i in range(4):
            dvo(DV.bn_stats(rsc[:, i * 6:(i + 1) * 6], x_ap[:, i * 512:(i + 1) * 512]))
        sdv()
        dvo(DV.bn_aggr(stats[:, 0:2], rsc[:, 0:24]))
        sdv()
        dvo(DV.tensor_scalar_mul(stats[:, 2:3], stats[:, 0:1], -1.0))
        e = dvo(DV.tensor_scalar_add(stats[:, 3:4], stats[:, 1:2], LN_EPS))
        k.wait(AC, e)
        e1 = ac.mark(AC.activation(stats[:, 5:6], stats[:, 3:4], AF.Ln))
        sac()
        e2 = ac.mark(AC.activation(stats[:, 6:7], stats[:, 5:6], AF.Exp, scale=-0.5))
        k.wait(DV, e2)
        if f32_ap is not None:
            dvo(DV.tensor_scalar(f32_ap, x_ap, stats[:, 2:3], stats[:, 6:7], ALU.add, ALU.mult))
            sdv()
            return dvo(DV.tensor_copy(out_bf_ap, f32_ap))
        return dvo(DV.tensor_scalar(out_bf_ap, x_ap, stats[:, 2:3], stats[:, 6:7],
                                    ALU.add, ALU.mult))

    def transpose16(src_bf, dst_view3, tcol, e_src):
        e_te = None
        e_dv = None
        for half in range(2):
            ti = tb_i[0] % 2
            tb_i[0] += 1
            tb = tbs_ps[ti]
            k.wait(TE, tb_evt[ti])
            k.wait(TE, e_src)
            k.wait(TE, e_idm)
            for j in range(8):
                cc = half * 8 + j
                ins = TE.transpose(tb[:, j * 128:(j + 1) * 128],
                                   src_bf[:, cc * 128:(cc + 1) * 128], idm_s[:, :])
            e_te = te.mark(ins)
            k.wait(DV, e_te)
            e_dv = dv.mark(DV.tensor_copy(
                dst_view3[:, half * 8:half * 8 + 8, tcol:tcol + 128],
                tb.ap().rearrange("p (j t) -> p j t", t=128)))
            tb_evt[ti] = e_dv
        return e_dv, e_te

    hfm_v = hfm.ap().rearrange("p (cc t) -> p cc t", t=TOK)

    k.wait(DV, e_xres0)
    for tc in range(TC):
        e_h = ln_chunk(xres[:, tc * D:(tc + 1) * D], htm_scr[:, :], guard=te.evt())
        transpose16(htm_scr, hfm_v, tc * 128, e_h)
    e_hfm = dv.evt()

    big_arena = ExitStack()
    obf = k.sb("obf", [128, NH * TOK], BF16, es=big_arena)
    wqkv = Stream(k, "wqkv", CC * 256, nbuf=3, es=big_arena, initial_free=snap())
    tbs = Stream(k, "tbs", 2 * 768, nbuf=2, es=big_arena, initial_free=snap())
    agsem = Ctr(k, "agsem", 16)
    ccs = Ctr(k, "ccs", 1)
    gsem = [Ctr(k, f"gsem{i}", 16) for i in range(2)]
    with ExitStack() as arena:
        ktm1 = k.sb("ktm1", [128, TC * BLK], BF16, es=arena)
        vtm1 = k.sb("vtm1", [128, TC * BLK], BF16, es=arena)
        kscr = k.sb("kscr", [128, 256], BF16, es=arena)
        wkv1 = Stream(k, "wkv1", CC * 256, es=arena, initial_free=snap())
        e_kscr = None
        for hg in range(8):
            skp, ekp = wkv1.load(wk_r[:, :, hg * 256:(hg + 1) * 256])
            svp, evp = wkv1.load(wv_r[:, :, hg * 256:(hg + 1) * 256])
            last_k = None
            for tsub in range(TC):
                b, bank = pool.get(TE)
                k.wait(TE, ekp)
                k.wait(TE, e_hfm)
                for cc in range(CC):
                    ins = TE.matmul(bank[:, 0:256],
                                    hfm[:, cc * TOK + tsub * 128: cc * TOK + tsub * 128 + 128],
                                    wkv1.bufs[skp][:, cc * 256:(cc + 1) * 256],
                                    start=(cc == 0), stop=(cc == CC - 1))
                e_mm = te.mark(ins)
                k.wait(AC, e_mm)
                k.wait(AC, e_kscr)
                e_s = ac.mark(AC.activation(kscr[:, :], bank[:, 0:256], AF.Silu))
                k.wait(DV, e_s)
                k.wait(DV, e_kdc)
                last = None
                for hh in range(2):
                    h = hg * 2 + hh
                    last = DV.tensor_scalar_mul(
                        ktm1[:, tsub * BLK + hh * 128: tsub * BLK + hh * 128 + 128],
                        kscr[:, hh * 128:(hh + 1) * 128],
                        kdc_s[:, 2 * h + (tsub % 2): 2 * h + (tsub % 2) + 1])
                e_kscr = dv.mark(last)
                last_k = e_kscr
                pool.put(b, e_kscr)
            wkv1.done(skp, last_k)
            last_v = None
            for tsub in range(TC):
                b, bank = pool.get(TE)
                k.wait(TE, evp)
                k.wait(TE, e_hfm)
                for cc in range(CC):
                    ins = TE.matmul(bank[:, 0:256],
                                    hfm[:, cc * TOK + tsub * 128: cc * TOK + tsub * 128 + 128],
                                    wkv1.bufs[svp][:, cc * 256:(cc + 1) * 256],
                                    start=(cc == 0), stop=(cc == CC - 1))
                e_mm = te.mark(ins)
                k.wait(AC, e_mm)
                last_v = ac.mark(AC.activation(
                    vtm1[:, tsub * BLK:(tsub + 1) * BLK], bank[:, 0:256], AF.Silu))
                pool.put(b, last_v)
            wkv1.done(svp, last_v)
            e_kv = [dv.evt(), ac.evt()]
            for hh in range(2):
                h = hg * 2 + hh
                for blk in range(NB):
                    b, bank = pool.get(TE)
                    k.wait(TE, e_kv)
                    for nsub in range(2):
                        tsub = blk * 2 + nsub
                        ins = TE.matmul(bank[:, 0:128],
                                        ktm1[:, tsub * BLK + hh * 128: tsub * BLK + hh * 128 + 128],
                                        vtm1[:, tsub * BLK + hh * 128: tsub * BLK + hh * 128 + 128],
                                        start=(nsub == 0), stop=(nsub == 1))
                    e_mm = te.mark(ins)
                    k.wait(DV, e_mm)
                    k.wait(DV, e_S0)
                    sdv()
                    e_u = dv.mark(DV.scalar_tensor_tensor(
                        Sst[:, h * HD:(h + 1) * HD], Sst[:, h * HD:(h + 1) * HD],
                        bdk[h], bank[:, 0:128], ALU.mult, ALU.add))
                    pool.put(b, e_u)
            if hg == 3:
                k.wait(GP, dv.evt())
                e_st0 = agsem.mark(GP.dma_start(out=agin0[:, :], in_=Sst[:, 0:8 * HD]))
                k.wait(GP, e_st0)
                e_cc0 = ccs.mark(GP.collective_compute(
                    "AllGather", ALU.bypass,
                    replica_groups=[[0, 1, 2, 3], [4, 5, 6, 7]],
                    ins=[agin0.ap().opt()], outs=[agout0.ap().opt()]))

    pf = {}
    pf[(0, 'q')] = wqkv.load(wq_r[:, :, 0:256])
    pf[(0, 'k')] = wqkv.load(wk_r[:, :, 0:256])
    pf[(0, 'v')] = wqkv.load(wv_r[:, :, 0:256])
    pf[(0, 't')] = tbs.load(tbl_r[0:2, :, :].rearrange("h p c -> p h c"))

    k.wait(GP, dv.evt())
    e_st1 = agsem.mark(GP.dma_start(out=agin1[:, :], in_=Sst[:, 8 * HD:]))
    k.wait(GP, e_st1)
    e_cc1 = ccs.mark(GP.collective_compute(
        "AllGather", ALU.bypass,
        replica_groups=[[0, 1, 2, 3], [4, 5, 6, 7]],
        ins=[agin1.ap().opt()], outs=[agout1.ap().opt()]))
    def combine_half(half, e_cc, agout_h, gb_tensor):
        gev = None
        k.wait(SY, e_cc)
        for jj in range(4):
            k.wait(SY, gev)
            e_g = gsem[half].mark(SY.dma_start(out=gb_tensor[:, :],
                                               in_=agout_h[jj * 128:(jj + 1) * 128, :]))
            k.wait(DV, e_g)
            k.wait(DV, e_cf)
            k.wait(DV, agsem.evt())
            sdv()
            last = None
            for hh in range(8):
                h = half * 8 + hh
                col = cf_s[:, jj * NH + h: jj * NH + h + 1]
                gb = gb_tensor[:, hh * HD:(hh + 1) * HD]
                if jj == 0:
                    last = DV.tensor_scalar_mul(Sst[:, h * HD:(h + 1) * HD], gb, col)
                else:
                    last = DV.scalar_tensor_tensor(
                        Sst[:, h * HD:(h + 1) * HD], gb, col,
                        Sst[:, h * HD:(h + 1) * HD], ALU.mult, ALU.add)
            gev = dv.mark(last)

    combine_half(0, e_cc0, agout0, osq_scr)

    if debug:
        k.wait(SY, dv.evt())
        _e = sto.mark(SY.dma_start(out=dbg["d_S"][:, :], in_=Sst[:, :]))
        for _eng in (TE, DV, AC):
            k.wait(_eng, _e)

    with big_arena as arena:
        with ExitStack() as sub1:
            qfm = k.sb("qfm", [128, 2 * TOK], BF16, es=sub1)
            kfm = k.sb("kfm", [128, 2 * TOK], BF16, es=sub1)
            vtm = k.sb("vtma", [128, TC * BLK], BF16, es=sub1)
            kptm = k.sb("kptma", [128, TC * BLK], BF16, es=sub1)


            prev_ret = None
            for g in range(8):
                if g == 2:
                    combine_half(1, e_cc1, agout1, g_scr)
                h0 = 2 * g
                sq, eq_ = pf.pop((g, 'q'))
                sk, ek_ = pf.pop((g, 'k'))
                stb, etb = pf.pop((g, 't'))

                for (ss, ee, dst) in [(sq, eq_, qfm), (sk, ek_, kfm)]:
                    last_ev = None
                    for hh in range(2):
                        for th in range(2):
                            b, bank = pool.get(TE)
                            k.wait(TE, ee)
                            k.wait(TE, e_hfm)
                            for cc in range(CC):
                                ins = TE.matmul(bank[:, 0:512],
                                          wqkv.bufs[ss][:, cc * 256 + hh * 128: cc * 256 + hh * 128 + 128],
                                          hfm[:, cc * TOK + th * 512: cc * TOK + th * 512 + 512],
                                          start=(cc == 0), stop=(cc == CC - 1))
                            e_mm = te.mark(ins)
                            k.wait(AC, e_mm)
                            last_ev = ac.mark(AC.activation(
                                dst[:, hh * TOK + th * 512: hh * TOK + th * 512 + 512],
                                bank[:, 0:512], AF.Silu))
                            pool.put(b, last_ev)
                    wqkv.done(ss, last_ev)
                    if ss == sq and g + 1 < 8:
                        pf[(g + 1, 'q')] = wqkv.load(wq_r[:, :, (h0 + 2) * 128:(h0 + 4) * 128])
                        pf[(g + 1, 't')] = tbs.load(tbl_r[h0 + 2:h0 + 4, :, :].rearrange("h p c -> p h c"))
                    elif ss == sk and g + 1 < 8:
                        pf[(g + 1, 'k')] = wqkv.load(wk_r[:, :, (h0 + 2) * 128:(h0 + 4) * 128])
                e_qk = ac.evt()

                sv, ev_ = pf.pop((g, 'v'))
                last_ev = None
                for tsub in range(TC):
                    b, bank = pool.get(TE)
                    k.wait(TE, ev_)
                    k.wait(TE, e_hfm)
                    for cc in range(CC):
                        ins = TE.matmul(bank[:, 0:BLK],
                                  hfm[:, cc * TOK + tsub * 128: cc * TOK + tsub * 128 + 128],
                                  wqkv.bufs[sv][:, cc * BLK:(cc + 1) * BLK],
                                  start=(cc == 0), stop=(cc == CC - 1))
                    e_mm = te.mark(ins)
                    k.wait(AC, e_mm)
                    last_ev = ac.mark(AC.activation(
                        vtm[:, tsub * BLK:(tsub + 1) * BLK], bank[:, 0:BLK], AF.Silu))
                    pool.put(b, last_ev)
                wqkv.done(sv, last_ev)
                if g + 1 < 8:
                    pf[(g + 1, 'v')] = wqkv.load(wv_r[:, :, (h0 + 2) * 128:(h0 + 4) * 128])
                e_v = ac.evt()

                for tsub in range(TC):
                    ti = tb_i[0] % 2
                    tb_i[0] += 1
                    tb = tbs_ps[ti]
                    k.wait(TE, tb_evt[ti])
                    k.wait(TE, e_qk)
                    for hh in range(2):
                        ins = TE.transpose(tb[:, hh * 128:(hh + 1) * 128],
                                           kfm[:, hh * TOK + tsub * 128: hh * TOK + tsub * 128 + 128],
                                           idm_s[:, :])
                    e_t = te.mark(ins)
                    k.wait(DV, e_t)
                    k.wait(DV, e_kdc)
                    last = None
                    for hh in range(2):
                        h = h0 + hh
                        last = DV.tensor_scalar_mul(
                            kptm[:, tsub * BLK + hh * 128: tsub * BLK + hh * 128 + 128],
                            tb[:, hh * 128:(hh + 1) * 128],
                            kdc_s[:, 2 * h + (tsub % 2): 2 * h + (tsub % 2) + 1])
                    e_kp = dv.mark(last)
                    tb_evt[ti] = e_kp
                e_kp = dv.evt()

                tbuf = tbs.bufs[stb]
                iters = [(hh, blk) for hh in range(2) for blk in range(NB)]
                c1ev = [None] * len(iters)
                qqev = [None] * len(iters)
                c1bank = [None] * len(iters)

                def stage_a(i):
                    hh, blk = iters[i]
                    par = i % 2
                    k.wait(DV, etb)
                    k.wait(DV, bprev[0])
                    qqev[i] = dv.mark(DV.tensor_tensor(
                        qq_scr[:, par * BLK:(par + 1) * BLK],
                        qfm[:, hh * TOK + blk * BLK: hh * TOK + (blk + 1) * BLK],
                        tbuf[:, hh * 768: hh * 768 + 256], ALU.mult))
                    b1, bank1 = pool.get(TE)
                    k.wait(TE, e_qk)
                    for nsub in range(2):
                        ins = TE.matmul(bank1[:, nsub * 256:(nsub + 1) * 256],
                                  kfm[:, hh * TOK + blk * BLK + nsub * 128:
                                      hh * TOK + blk * BLK + nsub * 128 + 128],
                                  qfm[:, hh * TOK + blk * BLK: hh * TOK + (blk + 1) * BLK],
                                  start=True, stop=True)
                    e_c1 = te.mark(ins)
                    k.wait(DV, e_c1)
                    last = None
                    for nsub in range(2):
                        last = DV.tensor_tensor(
                            c1_scr[:, par * 512 + nsub * 256: par * 512 + (nsub + 1) * 256],
                            bank1[:, nsub * 256:(nsub + 1) * 256],
                            tbuf[:, hh * 768 + 256 + nsub * 256:
                                 hh * 768 + 256 + (nsub + 1) * 256],
                            ALU.mult)
                    c1ev[i] = dv.mark(last)
                    pool.put(b1, c1ev[i])

                bprev = [prev_ret]

                sbfev = [None] * len(iters)

                def stage_b_pre(i):
                    hh, blk = iters[i]
                    par = i % 2
                    h = h0 + hh
                    sdv()
                    sbfev[i] = dv.mark(DV.tensor_copy(
                        sbf_scr[:, par * HD:(par + 1) * HD], Sst[:, h * HD:(h + 1) * HD]))

                def stage_b(i):
                    hh, blk = iters[i]
                    par = i % 2
                    h = h0 + hh
                    e_sbf = sbfev[i]
                    b2, bank2 = pool.get(TE)
                    k.wait(TE, c1ev[i])
                    k.wait(TE, e_sbf)
                    k.wait(TE, qqev[i])
                    k.wait(TE, e_v)
                    for nsub in range(2):
                        tsub = blk * 2 + nsub
                        TE.matmul(bank2[:, 0:256],
                                  vtm[:, tsub * BLK + hh * 128: tsub * BLK + hh * 128 + 128],
                                  c1_scr[:, par * 512 + nsub * 256: par * 512 + (nsub + 1) * 256],
                                  start=(nsub == 0), stop=False)
                    ins = TE.matmul(bank2[:, 0:256], sbf_scr[:, par * HD:(par + 1) * HD],
                                    qq_scr[:, par * BLK:(par + 1) * BLK],
                                    start=False, stop=True)
                    e_o = te.mark(ins)
                    bprev[0] = e_o
                    k.wait(DV, e_o)
                    e_oe = dv.mark(DV.tensor_copy(
                        obf[:, h * TOK + blk * BLK: h * TOK + (blk + 1) * BLK],
                        bank2[:, 0:256]))
                    pool.put(b2, e_oe)
                    b3, bank3 = pool.get(TE)
                    k.wait(TE, e_kp)
                    for nsub in range(2):
                        tsub = blk * 2 + nsub
                        ins = TE.matmul(bank3[:, 0:128],
                                  kptm[:, tsub * BLK + hh * 128: tsub * BLK + hh * 128 + 128],
                                  vtm[:, tsub * BLK + hh * 128: tsub * BLK + hh * 128 + 128],
                                  start=(nsub == 0), stop=(nsub == 1))
                    e_cb = te.mark(ins)
                    k.wait(DV, e_cb)
                    sdv()
                    e_su = dv.mark(DV.scalar_tensor_tensor(
                        Sst[:, h * HD:(h + 1) * HD], Sst[:, h * HD:(h + 1) * HD],
                        bdk[h], bank3[:, 0:128], ALU.mult, ALU.add))
                    pool.put(b3, e_su)

                stage_a(0)
                for i in range(len(iters)):
                    stage_b_pre(i)
                    if i + 1 < len(iters):
                        stage_a(i + 1)
                    stage_b(i)
                prev_ret = bprev[0]
                tbs.done(stb, dv.evt())
            e_obf = dv.evt()

            if debug:
                k.wait(SY, e_obf)
                sto.mark(SY.dma_start(out=dbg["d_hfm"][:, :], in_=hfm[:, :]))
                _e = sto.mark(SY.dma_start(out=dbg["d_obf"][:, :], in_=obf[:, :]))
                for _eng in (TE, DV, AC):
                    k.wait(_eng, _e)

            br0, bankr0 = pool.reserve(TE)
            br1, bankr1 = pool.reserve(TE)
            e_sqprev = None
            e_gprev = None
            pg = wqkv.load(wga_r[:, :, 0:256])
            for h in range(NH):
                if h % 2 == 0:
                    sg, eg_ = pg
                    if h + 2 < NH:
                        pg = wqkv.load(wga_r[:, :, (h + 2) * 128:(h + 4) * 128])
                hh = h % 2
                k.wait(DV, e_obf)
                k.wait(DV, e_sqprev)
                sdv()
                e_sq = dv.mark(DV.tensor_tensor(
                    osq_scr[:, :], obf[:, h * TOK:(h + 1) * TOK],
                    obf[:, h * TOK:(h + 1) * TOK], ALU.mult))
                k.wait(TE, e_sq)
                k.wait(TE, e_onc)
                TE.matmul(bankr0[0:1, 0:512], onc_s[:, :], osq_scr[:, 0:512],
                          start=(h == 0), stop=(h == NH - 1))
                ins = TE.matmul(bankr1[0:1, 0:512], onc_s[:, :], osq_scr[:, 512:1024],
                                start=(h == 0), stop=(h == NH - 1))
                e_sqprev = te.mark(ins)
                for th in range(2):
                    b, bank = pool.get(TE)
                    k.wait(TE, eg_)
                    k.wait(TE, e_hfm)
                    for cc in range(CC):
                        ins = TE.matmul(bank[:, 0:512],
                                  wqkv.bufs[sg][:, cc * 256 + hh * 128: cc * 256 + hh * 128 + 128],
                                  hfm[:, cc * TOK + th * 512: cc * TOK + th * 512 + 512],
                                  start=(cc == 0), stop=(cc == CC - 1))
                    e_mm = te.mark(ins)
                    k.wait(AC, e_mm)
                    k.wait(AC, e_gprev)
                    e_g = ac.mark(AC.activation(g_scr[:, th * 512:(th + 1) * 512],
                                                bank[:, 0:512], AF.Sigmoid))
                    pool.put(b, e_g)
                if hh == 1:
                    wqkv.done(sg, ac.evt())
                k.wait(DV, ac.evt())
                k.wait(DV, e_nwc)
                sdv()
                e_gprev = dv.mark(DV.scalar_tensor_tensor(
                    obf[:, h * TOK:(h + 1) * TOK], obf[:, h * TOK:(h + 1) * TOK],
                    nwc_s[:, h:h + 1], g_scr[:, :], ALU.mult, ALU.mult))
            e_ofin = dv.evt()
            k.wait(DV, e_sqprev)
            dv.mark(DV.tensor_scalar_mul(r_row[0:1, 0:512], bankr0[0:1, 0:512], 1.0 / D))
            dv.mark(DV.tensor_scalar_mul(r_row[0:1, 512:1024], bankr1[0:1, 0:512], 1.0 / D))
            sdv()
            e_rr = dv.mark(DV.tensor_scalar_add(r_row[0:1, :], r_row[0:1, :], RMS_EPS))
            pool.put(br0, e_rr)
            pool.put(br1, e_rr)
            bb, bankb = pool.get(TE)
            k.wait(TE, e_rr)
            k.wait(TE, e_onr)
            for tsub in range(TC):
                ins = TE.matmul(bankb[:, tsub:tsub + 1],
                                r_row[0:1, tsub * 128:(tsub + 1) * 128],
                                onr_s[0:1, 0:1], start=True, stop=True)
            e_b = te.mark(ins)
            k.wait(DV, e_b)
            e_vc = dv.mark(DV.tensor_copy(rcol2_s[:, 0:TC], bankb[:, 0:TC]))
            pool.put(bb, e_vc)
            k.wait(AC, e_vc)
            ac.mark(AC.activation(rcol_s[:, 0:TC], rcol2_s[:, 0:TC], AF.Ln))
            sac()
            e_rc = ac.mark(AC.activation(rcol2_s[:, 0:TC], rcol_s[:, 0:TC], AF.Exp, scale=-0.5))

        wos = Stream(k, "wos", CC * 256, es=arena, initial_free=snap())
        for cs in range(8):
            so, eo_ = wos.load(wout_r[:, :, cs * 256:(cs + 1) * 256])
            for tsub in range(TC):
                b, bank = pool.get(TE)
                k.wait(TE, eo_)
                k.wait(TE, e_ofin)
                for hc in range(NH):
                    ins = TE.matmul(bank[:, 0:256],
                              obf[:, hc * TOK + tsub * 128: hc * TOK + tsub * 128 + 128],
                              wos.bufs[so][:, hc * 256:(hc + 1) * 256],
                              start=(hc == 0), stop=(hc == NH - 1))
                e_mm = te.mark(ins)
                k.wait(DV, e_mm)
                k.wait(DV, e_rc)
                sdv()
                e_acc = dv.mark(DV.scalar_tensor_tensor(
                    xres[:, tsub * D + cs * 256: tsub * D + cs * 256 + 256],
                    bank[:, 0:256], rcol2_s[:, tsub:tsub + 1],
                    xres[:, tsub * D + cs * 256: tsub * D + cs * 256 + 256],
                    ALU.mult, ALU.add))
                pool.put(b, e_acc)
            wos.done(so, dv.evt())
    e_x1 = dv.evt()

    if debug:
        k.wait(SY, e_x1)
        _e = sto.mark(SY.dma_start(out=dbg["d_x1"][:, :], in_=xres[:, :]))
        for _eng in (TE, DV, AC):
            k.wait(_eng, _e)

    with ExitStack() as arena:
        hf32_scr = k.sb("hf32_scr", [128, D], F32, es=arena)
        hlotm_scr = k.sb("hlotm", [128, D], BF16, es=arena)
        wgf_s = k.sb("wgf_s", [128, CC * E], F32, es=arena)
        whi = k.sb("whi", [128, CC * E], BF16, es=arena)
        wlo = k.sb("wlo", [128, CC * E], BF16, es=arena)

        k.wait(SY, snap())
        wgfsem = Ctr(k, "wgfsem", 16)
        e_wgf = wgfsem.mark(SY.dma_start(out=wgf_s[:, :], in_=wgf_r[:, :, :]))

        with ExitStack() as sub1:
            hlo = k.sb("hlo", [128, CC * TOK], BF16, es=sub1)
            hlo_v = hlo.ap().rearrange("p (cc t) -> p cc t", t=TOK)

            for tc in range(TC):
                k.wait(DV, e_x1)
                e_h = ln_chunk(xres[:, tc * D:(tc + 1) * D], htm_scr[:, :],
                               guard=te.evt(), f32_ap=hf32_scr[:, :])
                sdv()
                e_lo = dv.mark(DV.tensor_tensor(hlotm_scr[:, :], hf32_scr[:, :],
                                                htm_scr[:, :], ALU.subtract))
                transpose16(htm_scr, hfm_v, tc * 128, e_h)
                transpose16(hlotm_scr, hlo_v, tc * 128, e_lo)
            e_hfm2 = dv.evt()

            k.wait(DV, e_wgf)
            e_whi = dv.mark(DV.tensor_copy(whi[:, :], wgf_s[:, :]))
            sdv()
            e_wlo = dv.mark(DV.tensor_tensor(wlo[:, :], wgf_s[:, :], whi[:, :], ALU.subtract))

            k.wait(DV, ac.evt())
            for tsub in range(TC):
                b, bank = pool.get(TE)
                k.wait(TE, e_whi)
                k.wait(TE, e_wlo)
                k.wait(TE, e_hfm2)
                first = True
                for pa, (lh, rh) in enumerate([(hfm, whi), (hfm, wlo), (hlo, whi)]):
                    for cc in range(CC):
                        ins = TE.matmul(bank[:, 0:E],
                                  lh[:, cc * TOK + tsub * 128: cc * TOK + tsub * 128 + 128],
                                  rh[:, cc * E:(cc + 1) * E],
                                  start=first, stop=(pa == 2 and cc == CC - 1))
                        first = False
                e_lg = te.mark(ins)
                L = bank[:, 0:E]
                k.wait(DV, e_lg)
                sdv()
                dv.mark(DV.reduce_max(stats[:, 0:1], L, axis=AX.X))
                sdv()
                dv.mark(DV.tensor_scalar(rsc[:, 0:E], L, stats[:, 0:1], None, ALU.is_equal))
                sdv()
                dv.mark(DV.scalar_tensor_tensor(rsc[:, 8:16], rsc[:, 0:E], -1e30, L,
                                                ALU.mult, ALU.add))
                sdv()
                dv.mark(DV.reduce_max(stats[:, 1:2], rsc[:, 8:16], axis=AX.X))
                sdv()
                dv.mark(DV.tensor_scalar(rsc[:, 0:E], L, stats[:, 1:2], None, ALU.is_ge))
                dv.mark(DV.tensor_scalar_mul(stats[:, 2:3], stats[:, 0:1], -1.0))
                sdv()
                e_nm1 = dv.mark(DV.tensor_tensor(stats[:, 3:4], stats[:, 1:2],
                                                 stats[:, 0:1], ALU.subtract))
                k.wait(AC, e_nm1)
                ac.mark(AC.activation(rsc[:, 16:24], L, AF.Exp, bias=stats[:, 2:3]))
                e_e2 = ac.mark(AC.activation(stats[:, 4:5], stats[:, 3:4], AF.Exp))
                k.wait(DV, e_e2)
                dv.mark(DV.tensor_scalar_add(stats[:, 5:6], stats[:, 4:5], 1.0))
                sdv()
                dv.mark(DV.reciprocal(stats[:, 6:7], stats[:, 5:6]))
                dv.mark(DV.tensor_tensor(rsc[:, 24:32], rsc[:, 16:24], rsc[:, 0:E], ALU.mult))
                sdv()
                e_w = dv.mark(DV.tensor_scalar_mul(wrt[:, tsub * E:(tsub + 1) * E],
                                                   rsc[:, 24:32], stats[:, 6:7]))
                pool.put(b, e_w)
            e_wrt = dv.evt()

            if debug:
                k.wait(SY, e_wrt)
                _e = sto.mark(SY.dma_start(out=dbg["d_wrt"][:, :], in_=wrt[:, :]))
                for _eng in (TE, DV, AC):
                    k.wait(_eng, _e)

        h1 = [k.sb(f"h1_{i}", [128, 8 * TOK], BF16, es=arena) for i in range(2)]
        w1s = Stream(k, "w1s", CC * 128, es=arena, initial_free=snap())
        w2s = Stream(k, "w2s", 8 * 512, es=arena, initial_free=snap())
        h1ev = [None, None]

        tasks = [(e, fs) for e in range(E + 1) for fs in range(4)]
        s1done = {}

        def emit_s1(i):
            e, fs = tasks[i]
            w1src = ws1_r if e == E else w1_r[e]
            sl = i % 2
            hb = h1[sl]
            last_s1 = None
            for fc in range(8):
                s1, e1_ = w1s.load(w1src[:, :, fs * 1024 + fc * 128: fs * 1024 + fc * 128 + 128])
                for th in range(2):
                    b, bank = pool.get(TE)
                    k.wait(TE, e1_)
                    k.wait(TE, e_hfm2)
                    k.wait(TE, h1ev[sl])
                    for cc in range(CC):
                        ins = TE.matmul(bank[:, 0:512],
                                  w1s.bufs[s1][:, cc * 128:(cc + 1) * 128],
                                  hfm[:, cc * TOK + th * 512: cc * TOK + th * 512 + 512],
                                  start=(cc == 0), stop=(cc == CC - 1))
                    e_mm = te.mark(ins)
                    k.wait(AC, e_mm)
                    last_s1 = ac.mark(AC.activation(
                        hb[:, fc * TOK + th * 512: fc * TOK + th * 512 + 512],
                        bank[:, 0:512], AF.Silu))
                    pool.put(b, last_s1)
                w1s.done(s1, last_s1)
            s1done[i] = last_s1

        def emit_s2(i):
            e, fs = tasks[i]
            w2src = ws2_r if e == E else w2_r[e]
            shared = (e == E)
            sl = i % 2
            hb = h1[sl]
            e_h1 = s1done[i]
            last_te = None
            for cs in range(4):
                s2, e2_ = w2s.load(w2src[:, fs * 8:(fs + 1) * 8, cs * 512:(cs + 1) * 512])
                for tsub in range(TC):
                    b, bank = pool.get(TE)
                    k.wait(TE, e2_)
                    k.wait(TE, e_h1)
                    for fc in range(8):
                        ins = TE.matmul(bank[:, 0:512],
                                  hb[:, fc * TOK + tsub * 128: fc * TOK + tsub * 128 + 128],
                                  w2s.bufs[s2][:, fc * 512:(fc + 1) * 512],
                                  start=(fc == 0), stop=(fc == 7))
                    e_mm = te.mark(ins)
                    last_te = e_mm
                    k.wait(DV, e_mm)
                    xr = xres[:, tsub * D + cs * 512: tsub * D + cs * 512 + 512]
                    sdv()
                    if shared:
                        e_acc = dv.mark(DV.tensor_tensor(xr, xr, bank[:, 0:512], ALU.add))
                    else:
                        k.wait(DV, e_wrt)
                        e_acc = dv.mark(DV.scalar_tensor_tensor(
                            xr, bank[:, 0:512], wrt[:, tsub * E + e: tsub * E + e + 1],
                            xr, ALU.mult, ALU.add))
                    pool.put(b, e_acc)
                    if shared and fs == 3:
                        k.wait(SY, e_acc)
                        sto.mark(SY.dma_start(
                            out=out_r[:, tsub, cs * 512:(cs + 1) * 512],
                            in_=xr))
                w2s.done(s2, dv.evt())
            h1ev[sl] = last_te

        emit_s1(0)
        for i in range(len(tasks)):
            if i + 1 < len(tasks):
                emit_s1(i + 1)
            emit_s2(i)

    k.wait(SY, sto.evt())

    k.es.close()
    return nc



def _host_inputs(L0_attn_x, Wqkv, Wg, Wout, norm_w, fWg, W1, W2, Ws1, Ws2):
    x = np.ascontiguousarray(np.asarray(L0_attn_x, dtype=np.float32).reshape(2 * 4096, D))
    Wqkv = np.asarray(Wqkv, dtype=np.float32).reshape(D, NH, 3, HD)
    wq = np.ascontiguousarray(Wqkv[:, :, 0, :].reshape(D, D))
    wk = np.ascontiguousarray(Wqkv[:, :, 1, :].reshape(D, D))
    wv = np.ascontiguousarray(Wqkv[:, :, 2, :].reshape(D, D))

    slopes = np.array(_slopes(), dtype=np.float64)
    m = np.arange(BLK, dtype=np.float64)
    qd = np.exp(-slopes[:, None] * (m[None, :] + 1.0))
    kd = np.exp(-slopes[:, None] * (BLK - 1.0 - m[None, :]))
    idx = m[None, :, None] - m[None, None, :]
    dd = np.where(idx >= 0, np.exp(-slopes[:, None, None] * idx), 0.0)
    ddT = np.ascontiguousarray(np.swapaxes(dd, 1, 2))

    tbl = np.zeros((NH, 128, 768), dtype=np.float32)
    tbl[:, :, 0:256] = qd[:, None, :]
    tbl[:, :, 256:512] = ddT[:, 0:128, :]
    tbl[:, :, 512:768] = ddT[:, 128:256, :]

    kdc = np.zeros((128, 2 * NH), dtype=np.float32)
    for h in range(NH):
        kdc[:, 2 * h] = kd[h, 0:128]
        kdc[:, 2 * h + 1] = kd[h, 128:256]

    nwc = np.ascontiguousarray(np.asarray(norm_w, dtype=np.float32).reshape(NH, 128).T)

    shared = dict(
        wq=wq, wk=wk, wv=wv,
        wga=np.ascontiguousarray(np.asarray(Wg, dtype=np.float32)),
        wout=np.ascontiguousarray(np.asarray(Wout, dtype=np.float32)),
        wgf=np.ascontiguousarray(np.asarray(fWg, dtype=np.float32)),
        w1=np.ascontiguousarray(np.asarray(W1, dtype=np.float32)),
        w2=np.ascontiguousarray(np.asarray(W2, dtype=np.float32)),
        ws1=np.ascontiguousarray(np.asarray(Ws1, dtype=np.float32)),
        ws2=np.ascontiguousarray(np.asarray(Ws2, dtype=np.float32)),
        tbl=tbl, kdc=kdc, nwc=nwc,
        idm=np.eye(128, dtype=np.float32),
        onc=np.ones((128, 1), dtype=np.float32),
        onr=np.ones((1, 128), dtype=np.float32),
    )
    bd = np.exp(-slopes * BLK)
    in_maps = []
    for c in range(8):
        b, j = c // 4, c % 4
        row0 = b * 4096
        xo = np.ascontiguousarray(x[row0 + j * TOK: row0 + (j + 1) * TOK])
        cf = np.zeros((128, 64), dtype=np.float32)
        for jj in range(j):
            cf[:, jj * NH:(jj + 1) * NH] = (bd ** (NB * (j - 1 - jj)))[None, :]
        in_maps.append(dict(shared, xo=xo, cf=cf))
    return in_maps


_NC_CACHE = {}


def kernel(L0_attn_x, L0_attn_Wqkv, L0_attn_Wg, L0_attn_Wout, L0_attn_norm_w,
           L0_ffn_Wg, L0_ffn_W1, L0_ffn_W2, L0_ffn_Ws1, L0_ffn_Ws2):
    in_maps = _host_inputs(L0_attn_x, L0_attn_Wqkv, L0_attn_Wg, L0_attn_Wout,
                           L0_attn_norm_w, L0_ffn_Wg, L0_ffn_W1, L0_ffn_W2,
                           L0_ffn_Ws1, L0_ffn_Ws2)
    if "nc" not in _NC_CACHE:
        _NC_CACHE["nc"] = build_nc(debug=False)
    nc = _NC_CACHE["nc"]
    res = run_bass_kernel_spmd(nc, in_maps, core_ids=list(range(8)))
    shards = [np.asarray(r["out"]) for r in res.results]
    full = np.concatenate(shards, axis=0).reshape(2, 4096, D)
    return full.astype(np.float32)
